# version 02 (final)
# baseline (speedup 1.0000x reference)
import numpy as np
from contextlib import ExitStack

import concourse.bass as bass
import concourse.mybir as mybir
from concourse.bass_utils import run_bass_kernel_spmd
from concourse.aot_env import allow_ici_all_to_all

F32 = mybir.dt.float32
BF16 = mybir.dt.bfloat16
U16 = mybir.dt.uint16
U32 = mybir.dt.uint32
AF = mybir.ActivationFunctionType
ALU = mybir.AluOpType
AXX = mybir.AxisListType.X

D = 1024
TF = 4096
TH = 2048
NCH_F = 32
NCH_H = 16

DBG = ()
STOP_AFTER = "mixer"


class Sem:
    def __init__(self, h):
        self.h = h
        self.n = 0
    def inc(self, k=1):
        self.n += k
        return self.n


class Ctx:
    pass


def _host_masks():
    s = np.arange(128)[:, None]
    t = np.arange(128)[None, :]
    def lvl(g):
        return ((s // (2 * g) == t // (2 * g)) & (s % (2 * g) < g)
                & (t % (2 * g) >= g)).astype(np.float32)
    m4d = ((s // 4 == t // 4) & (s < t)).astype(np.float32)
    return lvl(4), lvl(8), lvl(16), lvl(32), m4d


def _prep_host(inputs):
    I = {k: np.ascontiguousarray(np.asarray(v, dtype=np.float32))
         for k, v in inputs.items()}
    x = I["L0_attn_x"]
    m4, m8, m16, m32, m4d = _host_masks()
    half = D // 2
    inv_freq = (1.0 / 10000.0 ** (np.arange(half, dtype=np.float64) / half)).astype(np.float32)
    scale_vec = ((np.arange(0, D, 2) + 0.4 * D) / (1.4 * D)).astype(np.float64)
    ln_scale = np.log(scale_vec).astype(np.float32)

    def colsN(vec, nsl):
        return np.ascontiguousarray(np.asarray(vec, np.float32).reshape(nsl, 128).T)

    cat_embed = np.ascontiguousarray(
        np.concatenate([I["L1_ffn_down_embed"], I["L1_ffn_up_embed"]], axis=1))
    keys_r = np.ascontiguousarray(
        I["L1_ffn_keys"].transpose(2, 0, 3, 1).reshape(16, 128, 256))
    sel2 = np.zeros((128, 2), np.float32)
    sel2[0:64, 0] = 1.0
    sel2[64:128, 1] = 1.0
    common = dict(
        mu_all=np.ascontiguousarray(
            np.stack([colsN(I[f"L0_attn_mu_{m}"], 8) for m in "rkvwg"], 1)),
        ln1_w_c=colsN(I["L0_attn_ln1_w"], 8), ln2_w_c=colsN(I["L0_attn_ln2_w"], 8),
        wA=I["L0_attn_wA"],
        Wo=I["L0_attn_Wo"], ffn_w1=I["L0_attn_ffn_w1"], ffn_w2=I["L0_attn_ffn_w2"],
        moe_Wr=I["L0_ffn_Wr"],
        moe_br_rep=np.ascontiguousarray(np.broadcast_to(I["L0_ffn_br"], (128, 8))),
        moe_W1=I["L0_ffn_W1"], moe_W2=I["L0_ffn_W2"],
        moe_b1_c=np.ascontiguousarray(
            I["L0_ffn_b1"].reshape(8, 32, 128).transpose(0, 2, 1)),
        moe_b2_rep=np.ascontiguousarray(
            np.broadcast_to(I["L0_ffn_b2"][:, None, :], (8, 128, D))),
        WQ=I["L1_attn_W_Q"], WK=I["L1_attn_W_K"], WV=I["L1_attn_W_V"],
        w_q=I["L1_ffn_w_q"],
        bn_g_c=colsN(I["L1_ffn_bn_gamma"], 16), bn_b_c=colsN(I["L1_ffn_bn_beta"], 16),
        keys_r=keys_r, cat_embed=cat_embed,
        masks=np.ascontiguousarray(np.stack([m4, m8, m16, m32, m4d], 1)),
        lt_ones=np.triu(np.ones((128, 128), np.float32)),
        ident=np.eye(128, dtype=np.float32),
        invfreq_rep=np.ascontiguousarray(np.broadcast_to(inv_freq, (128, half))),
        lnscale_rep=np.ascontiguousarray(np.broadcast_to(ln_scale, (128, half))),
        sel2=sel2,
    )
    maps = []
    for c in range(8):
        b, hf = c // 2, c % 2
        hoff = hf * 512
        m = dict(common)
        m.update(
            x_full=x[b],
            x_my=np.ascontiguousarray(x[b, hf * TH:(hf + 1) * TH]),
            xprev=np.zeros((1, D), np.float32),
            Wr_my=np.ascontiguousarray(I["L0_attn_Wr"][:, hoff:hoff + 512]),
            Wk_my=np.ascontiguousarray(I["L0_attn_Wk"][:, hoff:hoff + 512]),
            Wv_my=np.ascontiguousarray(I["L0_attn_Wv"][:, hoff:hoff + 512]),
            Wg_my=np.ascontiguousarray(I["L0_attn_Wg"][:, hoff:hoff + 512]),
            wB_my=np.ascontiguousarray(I["L0_attn_wB"][:, hoff:hoff + 512]),
            w0_rep_my=np.ascontiguousarray(
                np.broadcast_to(I["L0_attn_w0"][hoff:hoff + 512], (128, 512))),
            u_my_c=colsN(I["L0_attn_u"].reshape(-1)[hoff:hoff + 512], 4),
            gnw_rep_my=np.ascontiguousarray(np.broadcast_to(
                I["L0_attn_gn_w"].reshape(-1)[hoff:hoff + 512], (128, 512))),
            gnb_rep_my=np.ascontiguousarray(np.broadcast_to(
                I["L0_attn_gn_b"].reshape(-1)[hoff:hoff + 512], (128, 512))),
            tpos_col=np.ascontiguousarray(
                (hf * TH + np.arange(TH, dtype=np.float32)).reshape(NCH_H, 128).T),
            ogrow_idx=np.ascontiguousarray(
                (hf * TF + np.arange(NCH_F * 128, dtype=np.uint32)).reshape(NCH_F, 128).T),
            worow_idx=np.ascontiguousarray(np.stack([
                (hf * TH + np.arange(TH, dtype=np.uint32)).reshape(NCH_H, 128).T,
                (TF + hf * TH + np.arange(TH, dtype=np.uint32)).reshape(NCH_H, 128).T,
            ], 2)),
        )
        maps.append(m)
    return maps


def build_program(dbg=(), stop_after=None):
    nc = bass.Bass("TRN2", target_bir_lowering=False)
    es = ExitStack()
    C = Ctx()
    C.nc, C.es = nc, es
    C.q = {e: [] for e in ("sync", "scalar", "vector", "tensor", "gpsimd")}
    C.dbg = set(dbg)
    C.dbg_outs = {}

    def sb(name, shape, dt=F32, scope=None):
        return (scope or es).enter_context(nc.sbuf_tensor(name, list(shape), dt))
    def psum(name, shape, dt=F32):
        return es.enter_context(nc.psum_tensor(name, list(shape), dt))
    def sem(name):
        return Sem(es.enter_context(nc.semaphore(name)))
    def on(eng, fn):
        C.q[eng].append(fn)
    C.sb, C.psum, C.sem, C.on = sb, psum, sem, on

    def din(name, shape, dt=F32):
        return nc.dram_tensor(name, list(shape), dt, kind="ExternalInput").ap()

    def dout(name, shape, dt=F32):
        C.dbg_outs[name] = nc.dram_tensor("dbg_" + name, list(shape), dt,
                                          kind="ExternalOutput").ap()
        return C.dbg_outs[name]
    C.dout = dout

    C.x_full = din("x_full", [TF, D])
    C.x_my = din("x_my", [TH, D])
    C.xprev = din("xprev", [1, D])
    C.mu_all = din("mu_all", [128, 5, 8])
    C.ln1_w_c = din("ln1_w_c", [128, 8])
    C.ln2_w_c = din("ln2_w_c", [128, 8])
    C.Wmy = {m: din(f"W{m}_my", [D, 512]) for m in "rkvg"}
    C.wA_in = din("wA", [D, 64])
    C.wB_my = din("wB_my", [64, 512])
    C.w0_rep_my = din("w0_rep_my", [128, 512])
    C.u_my_c = din("u_my_c", [128, 4])
    C.gnw_rep_my = din("gnw_rep_my", [128, 512])
    C.gnb_rep_my = din("gnb_rep_my", [128, 512])
    C.Wo_in = din("Wo", [D, D])
    C.ffn_w1_in = din("ffn_w1", [D, 4096])
    C.ffn_w2_in = din("ffn_w2", [4096, D])
    C.moe_Wr_in = din("moe_Wr", [D, 8])
    C.moe_br_rep = din("moe_br_rep", [128, 8])
    C.moe_W1_in = din("moe_W1", [8, D, 4096])
    C.moe_W2_in = din("moe_W2", [8, 4096, D])
    C.moe_b1_c = din("moe_b1_c", [8, 128, 32])
    C.moe_b2_rep = din("moe_b2_rep", [8, 128, D])
    C.WQ_in = din("WQ", [D, D])
    C.WK_in = din("WK", [D, D])
    C.WV_in = din("WV", [D, D])
    C.wq_in = din("w_q", [D, 2048])
    C.bn_g_c = din("bn_g_c", [128, 16])
    C.bn_b_c = din("bn_b_c", [128, 16])
    C.keys_in = din("keys_r", [16, 128, 256])
    C.cat_embed = din("cat_embed", [65536, 2 * D])
    C.masks_in = din("masks", [128, 5, 128])
    C.lt_in = din("lt_ones", [128, 128])
    C.id_in = din("ident", [128, 128])
    C.sel2_in = din("sel2", [128, 2])
    C.invfreq_rep = din("invfreq_rep", [128, 512])
    C.lnscale_rep = din("lnscale_rep", [128, 512])
    C.tpos_col = din("tpos_col", [128, NCH_H])
    C.ogrow_idx = din("ogrow_idx", [128, NCH_F], U32)
    C.worow_idx = din("worow_idx", [128, NCH_H, 2], U32)
    C.out = nc.dram_tensor("out", [TH, D], F32, kind="ExternalOutput").ap()

    C.cc_og_in = nc.dram_tensor("cc_og_in", [2 * TF, 512], BF16).ap()
    C.cc_og_out = nc.dram_tensor("cc_og_out", [2 * TF, 512], BF16).ap()

    build_mixer(C)

    fin = C.sem("fin")
    C.fin = fin
    if stop_after == "mixer":
        C.og_out = nc.dram_tensor("og_out", [2 * TF, 512], BF16,
                                  kind="ExternalOutput").ap()
        bt = C.sb("ogbounce", [128, 512], BF16)
        srcv = C.cc_og_out.rearrange("(a p) n -> a p n", p=128)
        dstv = C.og_out.rearrange("(a p) n -> a p n", p=128)
        def bounce(e, vcc=C.s_cc.n, base=fin.n):
            e.wait_ge(C.s_cc.h, vcc)
            for j in range(2 * TF // 128):
                if j > 0:
                    e.wait_ge(fin.h, base + 32 * j)
                e.dma_start(out=bt[:], in_=srcv[j]).then_inc(fin.h, 16)
                e.wait_ge(fin.h, base + 32 * j + 16)
                e.dma_start(out=dstv[j], in_=bt[:]).then_inc(fin.h, 16)
        on("sync", bounce)
        fin.inc(32 * (2 * TF // 128))
    else:
        C.es_mix.close()
        build_p34(C, stop_after)
        if stop_after not in ("p3", "p4"):
            C.es_p34.close()
            build_l1(C)

    emit_debug(C)

    def fwait(e):
        e.wait_ge(fin.h, fin.n)
    on("gpsimd", fwait)

    with nc.Block() as block:
        for eng in ("sync", "scalar", "vector", "tensor", "gpsimd"):
            deco = getattr(block, eng)
            for fn in C.q[eng]:
                deco(fn)
    nc.finalize()
    return nc, C.dbg_outs


def emit_debug(C):
    fin = C.fin
    for name in sorted(C.dbg):
        if name == "og_hbm":
            t = C.dout(name, [2 * TF, 512], BF16)
            src = C.cc_og_in
        elif name == "og_cc":
            t = C.dout(name, [2 * TF, 512], BF16)
            src = C.cc_og_out
        elif hasattr(C, name):
            tile = getattr(C, name)
            ap = tile.ap()
            t = C.dout(name, list(ap.shape), ap.dtype)
            src = ap
        else:
            continue
        def mk(t=t, src=src):
            def fn(e):
                e.wait_ge(C.s_cc.h, C.s_cc.n)
                e.dma_start(out=t, in_=src).then_inc(fin.h, 16)
            return fn
        C.on("sync", mk())
        fin.inc(16)


def build_mixer(C):
    sem, on = C.sem, C.on
    psum = C.psum
    C.es_mix = ExitStack()
    def sb(name, shape, dt=F32):
        return C.sb(name, shape, dt, scope=C.es_mix)

    C.c_masks = sb("c_masks", [128, 5, 128], BF16)
    C.c_lt = sb("c_lt", [128, 128], F32)
    C.c_id_b = C.sb("c_id_b", [128, 128], BF16)
    C.c_id_f = C.sb("c_id_f", [128, 128], F32)
    C.c_mu = sb("c_mu", [128, 5, 8], F32)
    C.c_ln1 = sb("c_ln1", [128, 8], F32)
    C.c_u = sb("c_u", [128, 4], F32)
    C.c_w0 = sb("c_w0", [128, 512], F32)
    C.c_gnw = sb("c_gnw", [128, 512], F32)
    C.c_gnb = sb("c_gnb", [128, 512], F32)
    C.c_zero1 = sb("c_zero1", [1, 512], BF16)
    C.c_sel2 = sb("c_sel2", [128, 2], BF16)
    C.c_eps6 = sb("c_eps6", [128, 1], F32)
    C.wb = {m: sb(f"wb_{m}", [128, 8, 512], BF16) for m in "rkvg"}
    C.wb_A = sb("wb_A", [128, 8, 64], BF16)
    C.wb_B = sb("wb_B", [64, 512], BF16)
    C.xb = C.sb("xb", [128, 3, D], F32)
    C.xc = C.sb("xc", [128, 2, D], F32)
    C.sq16 = C.sb("sq16", [128, D], BF16)
    C.tiny = C.sb("tiny", [128, 16], F32)
    C.htok = sb("htok", [128, 2, D], F32)
    C.hfm = sb("hfm", [128, 8, 129], F32)
    C.xxfm = sb("xxfm", [128, 8, 128], F32)
    C.mixfm = sb("mixfm", [128, 5, 8, 128], BF16)
    C.rfm = sb("rfm", [128, 2, 512], F32)
    C.kfm = sb("kfm", [128, 2, 512], F32)
    C.vtok = sb("vtok", [128, 2, 512], BF16)
    C.gtok = sb("gtok", [128, 512], BF16)
    C.ty = sb("ty", [128, 64], BF16)
    C.tyT = sb("tyT", [64, 128], BF16)
    C.wtmp = sb("wtmp", [128, 512], F32)
    C.expw = sb("expw", [128, 512], F32)
    C.chat = sb("chat", [128, 512], F32)
    C.csh = sb("csh", [128, 512], F32)
    C.crep = sb("crep", [128, 512], F32)
    C.esc = sb("esc", [128, 512], F32)
    C.ops_k = sb("ops_k", [128, 6, 512], BF16)
    C.ops_r = sb("ops_r", [128, 6, 512], BF16)
    C.ops_m = sb("ops_m", [128, 2, 512], BF16)
    C.ppiece = sb("ppiece", [128, 6, 128], BF16)
    C.dstmp = sb("dstmp", [128, 512], F32)
    C.k128tok = sb("k128tok", [128, 64], BF16)
    C.s_allt = sb("s_allt", [128, 4, 64], F32)
    C.s16 = sb("s16", [128, 4, 64], BF16)
    C.acol = sb("acol", [128, 4], F32)
    C.uterm = sb("uterm", [128, 512], F32)
    C.ruk = sb("ruk", [128, 512], BF16)
    C.cu_sb = sb("cu_sb", [2, 512], BF16)
    C.cuT = sb("cuT", [128, 8], F32)
    C.o_sb = sb("o_sb", [128, 512], F32)
    C.osq = sb("osq", [128, 512], F32)
    C.gstat = sb("gstat", [128, 24], F32)
    C.ogtok = sb("ogtok", [128, 512], BF16)
    C.ogfm = sb("ogfm", [128, 4, 128], BF16)
    C.hprev_sb = sb("hprev_sb", [1, D], F32)
    C.hprow = sb("hprow", [1, D], F32)
    C.ogidx = sb("ogidx", [128, NCH_F], U32)
    C.woidx = sb("woidx", [128, NCH_H, 2], U32)
    C.zrow = sb("zrow", [128, 512], BF16)

    C.psA = psum("psA", [128, 512], F32)
    C.psB = psum("psB", [128, 512], F32)
    C.psP = psum("psP", [128, 512], F32)
    C.psQ = psum("psQ", [128, 512], F32)
    C.psO = psum("psO", [128, 512], F32)
    C.psT0 = psum("psT0", [128, 128], F32)
    C.psT1 = psum("psT1", [128, 128], F32)
    C.psU = psum("psU", [128, 256], BF16)

    for n in ("s_const s_x s_av s_as s_h s_hT s_hfm s_mix s_pj s_pe s_cum "
              "s_ce s_eb s_sm s_se s_pv s_st s_oe s_og s_ogd s_cc s_zf "
              "s_vc s_sc s_ds").split():
        setattr(C, n, sem(n))

    def p0_sync(e):
        e.dma_start(out=C.ogidx[:], in_=C.ogrow_idx).then_inc(C.s_const.h, 16)
        e.dma_start(out=C.woidx.ap(), in_=C.worow_idx).then_inc(C.s_const.h, 16)
        for (dst, src) in [
            (C.c_lt[:], C.lt_in),
            (C.c_id_f[:], C.id_in), (C.c_mu.ap(), C.mu_all),
            (C.c_ln1[:], C.ln1_w_c), (C.c_u[:], C.u_my_c),
            (C.c_w0[:], C.w0_rep_my), (C.c_gnw[:], C.gnw_rep_my),
            (C.c_gnb[:], C.gnb_rep_my), (C.hprev_sb[:], C.xprev),
        ]:
            e.dma_start(out=dst, in_=src).then_inc(C.s_const.h, 16)
    on("sync", p0_sync)

    def p0_gp(e):
        e.dma_start(out=C.c_id_b[:], in_=C.id_in).then_inc(C.s_const.h, 16)
        e.dma_start(out=C.c_masks.ap(), in_=C.masks_in).then_inc(C.s_const.h, 16)
        e.dma_start(out=C.c_sel2[:], in_=C.sel2_in).then_inc(C.s_const.h, 16)
        for m in "rkvg":
            e.dma_start(out=C.wb[m].ap(),
                        in_=C.Wmy[m].rearrange("(a p) n -> p a n", p=128)) \
                .then_inc(C.s_const.h, 16)
        e.dma_start(out=C.wb_A.ap(),
                    in_=C.wA_in.rearrange("(a p) n -> p a n", p=128)) \
            .then_inc(C.s_const.h, 16)
        e.dma_start(out=C.wb_B[:], in_=C.wB_my).then_inc(C.s_const.h, 16)
        e.memset(C.ops_k.ap(), 0.0)
        e.memset(C.ops_r.ap(), 0.0)
        e.memset(C.ops_m.ap(), 0.0)
        e.memset(C.c_zero1[:], 0.0)
        e.memset(C.s_allt.ap(), 0.0)
        e.memset(C.s16.ap(), 0.0)
        e.memset(C.uterm[:], 0.0)
        e.memset(C.hfm.ap(), 0.0)
        e.memset(C.zrow[:], 0.0)
        e.memset(C.c_eps6[:], 1e-6).then_inc(C.s_const.h, 1)
    on("gpsimd", p0_gp)
    C.s_const.n = 20 * 16 + 1
    NCONST = C.s_const.n

    t = C.tiny
    vc0 = C.s_vc.n
    C.s_vc.inc(5)
    def p0v(e, vc0=vc0):
        e.wait_ge(C.s_const.h, NCONST)
        e.tensor_reduce(t[0:1, 0:1], C.hprev_sb[0:1, :], axis=AXX, op=ALU.add) \
            .then_inc(C.s_vc.h, 1)
        e.wait_ge(C.s_vc.h, vc0 + 1)
        e.tensor_scalar(t[0:1, 1:2], t[0:1, 0:1], 1.0 / D, None, ALU.mult) \
            .then_inc(C.s_vc.h, 1)
        e.wait_ge(C.s_vc.h, vc0 + 2)
        e.tensor_scalar(C.hprev_sb[0:1, :], C.hprev_sb[0:1, :], t[0:1, 1:2],
                        None, ALU.subtract).then_inc(C.s_vc.h, 1)
        e.wait_ge(C.s_vc.h, vc0 + 3)
        e.tensor_tensor(C.xc[0:1, 0, :], C.hprev_sb[0:1, :], C.hprev_sb[0:1, :],
                        ALU.mult).then_inc(C.s_vc.h, 1)
        e.wait_ge(C.s_vc.h, vc0 + 4)
        e.tensor_reduce(t[0:1, 2:3], C.xc[0:1, 0, :], axis=AXX, op=ALU.add) \
            .then_inc(C.s_vc.h, 1)
        e.wait_ge(C.s_vc.h, vc0 + 5)
        e.tensor_scalar(t[0:1, 3:4], t[0:1, 2:3], 1.0 / D, 1e-5, ALU.mult,
                        ALU.add).then_inc(C.s_av.h, 1)
    on("vector", p0v)
    C.s_av.inc()

    sc0 = C.s_sc.n
    C.s_sc.inc(1)
    def p0s(e, sc0=sc0):
        e.wait_ge(C.s_av.h, 1)
        e.activation(t[0:1, 4:5], t[0:1, 3:4], AF.Ln).then_inc(C.s_sc.h, 1)
        e.wait_ge(C.s_sc.h, sc0 + 1)
        e.activation(t[0:1, 5:6], t[0:1, 4:5], AF.Exp, scale=-0.5)
        e.activation(t[0:1, 6:7], t[0:1, 4:5], AF.Exp, scale=-1.0) \
            .then_inc(C.s_as.h, 1)
    on("scalar", p0s)
    C.s_as.inc()

    vc0 = C.s_vc.n
    C.s_vc.inc(1)
    def p0v2(e, vc0=vc0):
        e.wait_ge(C.s_as.h, 1)
        e.tensor_scalar(t[0:1, 7:8], t[0:1, 2:3], 1.0 / D, None, ALU.mult) \
            .then_inc(C.s_vc.h, 1)
        e.wait_ge(C.s_vc.h, vc0 + 1)
        e.scalar_tensor_tensor(t[0:1, 8:9], t[0:1, 7:8], t[0:1, 6:7],
                               C.c_eps6[0:1, :], ALU.mult, ALU.add) \
            .then_inc(C.s_av.h, 1)
    on("vector", p0v2)
    C.s_av.inc()

    sc0 = C.s_sc.n
    C.s_sc.inc(1)
    def p0s2(e, sc0=sc0):
        e.wait_ge(C.s_av.h, 2)
        e.activation(t[0:1, 9:10], t[0:1, 8:9], AF.Ln).then_inc(C.s_sc.h, 1)
        e.wait_ge(C.s_sc.h, sc0 + 1)
        e.activation(t[0:1, 10:11], t[0:1, 9:10], AF.Exp, scale=-0.5) \
            .then_inc(C.s_as.h, 1)
    on("scalar", p0s2)
    C.s_as.inc()

    vc0 = C.s_vc.n
    C.s_vc.inc(1)
    def p0v3(e, vc0=vc0):
        e.wait_ge(C.s_as.h, 2)
        e.tensor_tensor(t[0:1, 11:12], t[0:1, 5:6], t[0:1, 10:11], ALU.mult) \
            .then_inc(C.s_vc.h, 1)
        e.wait_ge(C.s_vc.h, vc0 + 1)
        e.tensor_scalar(C.hprow[0:1, :], C.hprev_sb[0:1, :], t[0:1, 11:12],
                        None, ALU.mult).then_inc(C.s_av.h, 1)
    on("vector", p0v3)
    C.s_av.inc()

    def p0t(e):
        e.wait_ge(C.s_av.h, 3)
        for sl in range(8):
            e.transpose(C.psT0[:, sl:sl + 1], C.hprow[0:1, 128 * sl:128 * sl + 128],
                        C.c_id_f[0:1, 0:1]).then_inc(C.s_hT.h, 1)
    on("tensor", p0t)
    C.s_hT.inc(8)

    def p0s3(e):
        e.wait_ge(C.s_hT.h, 8)
        for sl in range(8):
            e.activation(C.hfm[:, sl, 0:1], C.psT0[:, sl:sl + 1], AF.Copy,
                         scale=C.c_ln1[:, sl:sl + 1]) \
                .then_inc(C.s_hfm.h, 1 if sl == 7 else 0)
    on("scalar", p0s3)
    C.s_hfm.inc()

    def p0_zf(e, vconst=20 * 16 + 1):
        e.wait_ge(C.s_const.h, vconst)
        ogv = C.cc_og_in.rearrange("(a p) n -> a p n", p=128)
        for j in range(2 * TF // 128):
            e.dma_start(out=ogv[j], in_=C.zrow[:]).then_inc(C.s_zf.h, 16)
    on("sync", p0_zf)
    C.s_zf.n = (2 * TF // 128) * 16

    C.x_ch = C.x_full.rearrange("(c p) d -> c p d", p=128)
    for f in range(NCH_F):
        _mixer_chunk(C, f)

    def cc(e, vogd=None):
        e.wait_ge(C.s_ogd.h, C.s_ogd.n)
        e.wait_ge(C.s_zf.h, C.s_zf.n)
        e.collective_compute(
            "AllReduce", ALU.add,
            replica_groups=[[0, 1], [2, 3], [4, 5], [6, 7]],
            ins=[C.cc_og_in.tensor.ap().opt()],
            outs=[C.cc_og_out.tensor.ap().opt()],
        ).then_inc(C.s_cc.h, 1)
    on("gpsimd", cc)
    C.s_cc.inc()


def _mixer_chunk(C, f):
    on = C.on
    b3, b2 = f % 3, f % 2
    t = C.tiny

    def ld(e, f=f, b3=b3, vg=C.s_av.n):
        if f >= 3:
            e.wait_ge(C.s_av.h, vg - 8)
        e.dma_start(out=C.xb[:, b3, :], in_=C.x_ch[f]).then_inc(C.s_x.h, 16)
    on("sync", ld)
    vx = C.s_x.inc(16)

    vc0 = C.s_vc.n
    C.s_vc.inc(2)
    def av1(e, f=f, b3=b3, b2=b2, vx=vx, vhp=C.s_h.n, vc0=vc0):
        e.wait_ge(C.s_x.h, vx)
        if f >= 2:
            e.wait_ge(C.s_h.h, vhp - 1)
        e.tensor_reduce(t[:, 0:1], C.xb[:, b3, :], axis=AXX, op=ALU.add) \
            .then_inc(C.s_vc.h, 1)
        e.wait_ge(C.s_vc.h, vc0 + 1)
        e.tensor_scalar(t[:, 1:2], t[:, 0:1], 1.0 / D, None, ALU.mult) \
            .then_inc(C.s_vc.h, 1)
        e.wait_ge(C.s_vc.h, vc0 + 2)
        e.tensor_scalar(C.xc[:, b2, :], C.xb[:, b3, :], t[:, 1:2], None,
                        ALU.subtract).then_inc(C.s_av.h, 1)
    on("vector", av1)
    vav1 = C.s_av.inc()

    def as1(e, b2=b2, vav1=vav1):
        e.wait_ge(C.s_av.h, vav1)
        e.activation(C.sq16[:, :], C.xc[:, b2, :], AF.Square,
                     accum_out=t[:, 2:3]).then_inc(C.s_as.h, 1)
    on("scalar", as1)
    vas1 = C.s_as.inc()

    def av2(e, vas1=vas1):
        e.wait_ge(C.s_as.h, vas1)
        e.tensor_scalar(t[:, 7:8], t[:, 2:3], 1.0 / D, None, ALU.mult)
        e.tensor_scalar(t[:, 3:4], t[:, 2:3], 1.0 / D, 1e-5, ALU.mult, ALU.add) \
            .then_inc(C.s_av.h, 1)
    on("vector", av2)
    vav2 = C.s_av.inc()

    sc0 = C.s_sc.n
    C.s_sc.inc(1)
    def as2(e, vav2=vav2, sc0=sc0):
        e.wait_ge(C.s_av.h, vav2)
        e.activation(t[:, 4:5], t[:, 3:4], AF.Ln).then_inc(C.s_sc.h, 1)
        e.wait_ge(C.s_sc.h, sc0 + 1)
        e.activation(t[:, 5:6], t[:, 4:5], AF.Exp, scale=-0.5)
        e.activation(t[:, 6:7], t[:, 4:5], AF.Exp, scale=-1.0) \
            .then_inc(C.s_as.h, 1)
    on("scalar", as2)
    vas2 = C.s_as.inc()

    def av3(e, vas2=vas2):
        e.wait_ge(C.s_as.h, vas2)
        e.scalar_tensor_tensor(t[:, 8:9], t[:, 7:8], t[:, 6:7], C.c_eps6[:, :],
                               ALU.mult, ALU.add).then_inc(C.s_av.h, 1)
    on("vector", av3)
    vav3 = C.s_av.inc()

    sc0 = C.s_sc.n
    C.s_sc.inc(1)
    def as3(e, vav3=vav3, sc0=sc0):
        e.wait_ge(C.s_av.h, vav3)
        e.activation(t[:, 9:10], t[:, 8:9], AF.Ln).then_inc(C.s_sc.h, 1)
        e.wait_ge(C.s_sc.h, sc0 + 1)
        e.activation(t[:, 10:11], t[:, 9:10], AF.Exp, scale=-0.5) \
            .then_inc(C.s_as.h, 1)
    on("scalar", as3)
    vas3 = C.s_as.inc()

    vc0 = C.s_vc.n
    C.s_vc.inc(1)
    def av4(e, b2=b2, f=f, vas3=vas3, vhT=C.s_hT.n, vc0=vc0):
        e.wait_ge(C.s_as.h, vas3)
        if f >= 2:
            e.wait_ge(C.s_hT.h, vhT - 8)
        e.tensor_tensor(t[:, 11:12], t[:, 5:6], t[:, 10:11], ALU.mult) \
            .then_inc(C.s_vc.h, 1)
        e.wait_ge(C.s_vc.h, vc0 + 1)
        e.tensor_scalar(C.htok[:, b2, :], C.xc[:, b2, :], t[:, 11:12], None,
                        ALU.mult).then_inc(C.s_h.h, 1)
    on("vector", av4)
    vh = C.s_h.inc()

    vcol0 = C.s_hfm.n
    if f > 0:
        def hcol0(e, vmixp=C.s_mix.n):
            e.wait_ge(C.s_mix.h, vmixp)
            e.activation(C.hfm.ap()[:, :, 0:1], C.hfm.ap()[:, :, 128:129],
                         AF.Copy).then_inc(C.s_hfm.h, 1)
        on("scalar", hcol0)
        vcol0 = C.s_hfm.inc()

    vhT0 = C.s_hT.n
    def hT(e, b2=b2, vh=vh, vcol0=vcol0):
        e.wait_ge(C.s_h.h, vh)
        e.wait_ge(C.s_hfm.h, vcol0)
        for sl in range(8):
            if sl >= 2:
                e.wait_ge(C.s_hfm.h, vcol0 + sl - 1)
            e.transpose((C.psT0 if sl % 2 == 0 else C.psT1)[:, :],
                        C.htok[:, b2, 128 * sl:128 * sl + 128], C.c_id_f[:]) \
                .then_inc(C.s_hT.h, 1)
    on("tensor", hT)
    C.s_hT.inc(8)

    def hev(e, vhT0=vhT0):
        for sl in range(8):
            e.wait_ge(C.s_hT.h, vhT0 + sl + 1)
            e.activation(C.hfm[:, sl, 1:129],
                         (C.psT0 if sl % 2 == 0 else C.psT1)[:, :],
                         AF.Copy, scale=C.c_ln1[:, sl:sl + 1]) \
                .then_inc(C.s_hfm.h, 1)
    on("scalar", hev)
    C.s_hfm.inc(8)
    vhfm = C.s_hfm.n

    vcm = C.s_vc.n
    C.s_vc.inc(1)
    def mix(e, vhfm=vhfm, f=f, vpjp=C.s_pj.n, vcm=vcm):
        e.wait_ge(C.s_hfm.h, vhfm)
        if f >= 1:
            e.wait_ge(C.s_pj.h, vpjp)
        e.tensor_tensor(C.xxfm.ap(), C.hfm.ap()[:, :, 0:128],
                        C.hfm.ap()[:, :, 1:129], ALU.subtract) \
            .then_inc(C.s_vc.h, 1)
        e.wait_ge(C.s_vc.h, vcm + 1)
        last = None
        for mi in range(5):
            for sl in range(8):
                last = e.scalar_tensor_tensor(
                    C.mixfm[:, mi, sl, :], C.xxfm[:, sl, :],
                    C.c_mu[:, mi, sl:sl + 1], C.hfm[:, sl, 1:129],
                    ALU.mult, ALU.add)
        last.then_inc(C.s_mix.h, 1)
    on("vector", mix)
    vmix = C.s_mix.inc()

    def proj1(e, f=f, vmix=vmix, vpep=C.s_pe.n, vstp=C.s_st.n, vsmp=C.s_sm.n):
        e.wait_ge(C.s_mix.h, vmix)
        if f >= 1:
            e.wait_ge(C.s_pe.h, vpep)
            e.wait_ge(C.s_st.h, vstp)
        if f >= 2:
            e.wait_ge(C.s_sm.h, vsmp - 18)
        last = None
        for (mi, ps) in ((0, C.psA), (1, C.psB)):
            w = C.wb["rkvg"[mi]]
            for ms in range(4):
                for ks in range(8):
                    last = e.matmul(ps[:, 128 * ms:128 * ms + 128],
                                    w[:, ks, 128 * ms:128 * ms + 128],
                                    C.mixfm[:, mi, ks, :],
                                    start=(ks == 0), stop=(ks == 7))
        last.then_inc(C.s_pj.h, 1)
    on("tensor", proj1)
    vpj1 = C.s_pj.inc()

    def pev1(e, b2=b2, vpj1=vpj1):
        e.wait_ge(C.s_pj.h, vpj1)
        e.activation(C.rfm[:, b2, :], C.psA[:, :], AF.Copy)
        e.activation(C.kfm[:, b2, :], C.psB[:, :], AF.Copy).then_inc(C.s_pe.h, 1)
    on("scalar", pev1)
    vpe1 = C.s_pe.inc()

    def proj2(e, vpe1=vpe1):
        e.wait_ge(C.s_pe.h, vpe1)
        for ks in range(8):
            e.matmul(C.psA[:, :], C.mixfm[:, 2, ks, :], C.wb["v"][:, ks, :],
                     start=(ks == 0), stop=(ks == 7))
        last = None
        for ks in range(8):
            last = e.matmul(C.psB[:, :], C.mixfm[:, 4, ks, :],
                            C.wb["g"][:, ks, :], start=(ks == 0), stop=(ks == 7))
        last.then_inc(C.s_pj.h, 1)
    on("tensor", proj2)
    vpj2 = C.s_pj.inc()

    def pev2(e, b2=b2, vpj2=vpj2, voep=C.s_oe.n):
        e.wait_ge(C.s_pj.h, vpj2)
        if voep > 0:
            e.wait_ge(C.s_oe.h, voep)
        e.activation(C.vtok[:, b2, :], C.psA[:, :], AF.Copy)
        e.activation(C.gtok[:, :], C.psB[:, :], AF.Silu).then_inc(C.s_pe.h, 1)
    on("scalar", pev2)
    vpe2 = C.s_pe.inc()

    def proj3(e, vpe2=vpe2):
        e.wait_ge(C.s_pe.h, vpe2)
        last = None
        for ks in range(8):
            last = e.matmul(C.psA[:, 0:64], C.mixfm[:, 3, ks, :],
                            C.wb_A[:, ks, :], start=(ks == 0), stop=(ks == 7))
        last.then_inc(C.s_pj.h, 1)
    on("tensor", proj3)
    vpj3 = C.s_pj.inc()

    def pev3(e, vpj3=vpj3):
        e.wait_ge(C.s_pj.h, vpj3)
        e.activation(C.ty[:, :], C.psA[:, 0:64], AF.Tanh).then_inc(C.s_pe.h, 1)
    on("scalar", pev3)
    vpe3 = C.s_pe.inc()

    def proj4(e, vpe3=vpe3, vsep=C.s_se.n, f=f):
        e.wait_ge(C.s_pe.h, vpe3)
        if f >= 1:
            e.wait_ge(C.s_se.h, vsep)
        e.transpose(C.psU[0:64, 64:192], C.ty[:, :], C.c_id_b[:]) \
            .then_inc(C.s_pj.h, 1)
    on("tensor", proj4)
    vpj4 = C.s_pj.inc()

    def pev4(e, vpj4=vpj4):
        e.wait_ge(C.s_pj.h, vpj4)
        e.activation(C.tyT[0:64, :], C.psU[0:64, 64:192], AF.Copy) \
            .then_inc(C.s_pe.h, 1)
    on("scalar", pev4)
    vpe4 = C.s_pe.inc()

    def proj5(e, vpe4=vpe4):
        e.wait_ge(C.s_pe.h, vpe4)
        e.matmul(C.psB[:, :], C.tyT[0:64, :], C.wb_B[0:64, :],
                 start=True, stop=True).then_inc(C.s_pj.h, 1)
    on("tensor", proj5)
    vpj5 = C.s_pj.inc()

    def wev(e, vpj5=vpj5, vwexp_prev=C.s_as.n):
        e.wait_ge(C.s_pj.h, vpj5)
        e.tensor_tensor(C.wtmp[:, :], C.psB[:, :], C.c_w0[:, :], ALU.add) \
            .then_inc(C.s_pe.h, 1)
    on("vector", wev)
    vpe5 = C.s_pe.inc()

    def wexp(e, vpe5=vpe5, vcum_prev=C.s_cum.n):
        e.wait_ge(C.s_pe.h, vpe5)
        if vcum_prev > 0:
            e.wait_ge(C.s_cum.h, vcum_prev)
        e.activation(C.expw[:, :], C.wtmp[:, :], AF.Exp).then_inc(C.s_as.h, 1)
    on("scalar", wexp)
    vexp = C.s_as.inc()

    def cum(e, vexp=vexp):
        e.wait_ge(C.s_as.h, vexp)
        last = None
        for ds in range(4):
            last = e.matmul(C.psA[:, 128 * ds:128 * ds + 128],
                            C.expw[:, 128 * ds:128 * ds + 128], C.c_lt[:, :],
                            start=True, stop=True)
        last.then_inc(C.s_cum.h, 1)
    on("tensor", cum)
    vcum = C.s_cum.inc()

    def cumev(e, vcum=vcum, vsmp=C.s_sm.n, f=f):
        e.wait_ge(C.s_cum.h, vcum)
        if f >= 1:
            e.wait_ge(C.s_sm.h, vsmp)
        e.tensor_copy(C.chat[:, :], C.psA[:, :])
        psv = C.psA.ap().rearrange("p (a u) -> p a u", a=4)
        csv = C.csh.ap().rearrange("p (a u) -> p a u", a=4)
        e.memset(csv[:, :, 0:1], 0.0)
        e.tensor_copy(csv[:, :, 1:128], psv[:, :, 0:127])
        e.tensor_copy(t[:, 12:13], t[:, 11:12]).then_inc(C.s_ce.h, 1)
    on("vector", cumev)
    vce = C.s_ce.inc()

    vas0 = C.s_as.n
    veb0 = C.s_eb.n
    NEX = 14

    def half_view(tile_ap, gg, side):
        nblk = 128 // (2 * gg)
        v = tile_ap.rearrange("p (a b u) -> p a b u", a=4, b=nblk)
        return v[:, :, :, 0:gg] if side == 0 else v[:, :, :, gg:2 * gg]

    specs = []
    for li, gg in enumerate((4, 8, 16, 32, 64)):
        specs.append(("k", li, gg, "split"))
        specs.append(("r", li, gg, "split"))
    specs.append(("k", 5, None, "end"))
    specs.append(("r", 5, None, "glob"))
    specs.append(("mk", 0, None, "mid"))
    specs.append(("mr", 1, None, "mid"))

    esc_views = []
    for (kind, li, gg, mode) in specs:
        if mode == "split":
            esc_views.append(half_view(C.esc.ap(), gg, 0 if kind == "k" else 1))
        else:
            esc_views.append(C.esc.ap())

    vceb = C.s_vc.n
    C.s_vc.inc(15)

    def ebuild(e, f=f, b2=b2, vas0=vas0, vce=vce, vsmp=C.s_sm.n, vceb=vceb):
        e.wait_ge(C.s_ce.h, vce)
        if f >= 1:
            e.wait_ge(C.s_sm.h, vsmp)
        chv = C.chat.ap().rearrange("p (a u) -> p a u", a=4)
        crv = C.crep.ap()
        step = 0
        for (kind, li, gg, mode) in specs:
            if mode == "split":
                nrep = 128 // (2 * gg)
                src_ = chv[:, :, gg - 1::2 * gg]
                e.tensor_copy(
                    crv.rearrange("p (a c u) -> p a c u", a=4, c=nrep),
                    src_.rearrange("p a (c u) -> p a c u", u=1)
                        .broadcast_to([128, 4, nrep, 2 * gg])) \
                    .then_inc(C.s_vc.h, 1)
            elif mode == "end":
                e.tensor_copy(crv.rearrange("p (a u) -> p a u", a=4),
                              chv[:, :, 127:128].broadcast_to([128, 4, 128])) \
                    .then_inc(C.s_vc.h, 1)
            elif mode == "glob":
                e.memset(C.crep[:, :], 0.0).then_inc(C.s_vc.h, 1)
            elif mode == "mid" and kind == "mk":
                e.tensor_copy(
                    crv.rearrange("p (a c u) -> p a c u", a=4, c=32),
                    chv[:, :, 1::4].rearrange("p a (c u) -> p a c u", u=1)
                       .broadcast_to([128, 4, 32, 4])).then_inc(C.s_vc.h, 1)
            else:
                e.engine_nop().then_inc(C.s_vc.h, 1)
            e.wait_ge(C.s_vc.h, vceb + step + 1)
            if mode == "split":
                sidx = 0 if kind == "k" else 1
                ev = half_view(C.esc.ap(), gg, sidx)
                if kind == "k":
                    e.tensor_tensor(ev, half_view(C.chat.ap(), gg, 0),
                                    half_view(C.crep.ap(), gg, 0), ALU.subtract)
                else:
                    e.tensor_tensor(ev, half_view(C.crep.ap(), gg, 1),
                                    half_view(C.csh.ap(), gg, 1), ALU.subtract)
            else:
                if kind in ("k", "mk"):
                    e.tensor_tensor(C.esc[:, :], C.chat[:, :], C.crep[:, :],
                                    ALU.subtract)
                else:
                    e.tensor_tensor(C.esc[:, :], C.crep[:, :], C.csh[:, :],
                                    ALU.subtract)
            if mode == "mid":
                e.tensor_scalar(C.esc[:, :], C.esc[:, :], 35.0, None, ALU.min)
            e.tensor_copy(t[:, 13:14], t[:, 11:12]).then_inc(C.s_eb.h, 1)
            step += 1
            e.wait_ge(C.s_as.h, vas0 + step)
            if kind == "k" and mode == "split":
                dst = half_view(C.ops_k[:, li, :], gg, 0)
                srcv = half_view(C.kfm[:, b2, :], gg, 0)
                e.tensor_tensor(dst, srcv, half_view(C.esc.ap(), gg, 0), ALU.mult)
            elif kind == "r" and mode == "split":
                dst = half_view(C.ops_r[:, li, :], gg, 1)
                srcv = half_view(C.rfm[:, b2, :], gg, 1)
                e.tensor_tensor(dst, srcv, half_view(C.esc.ap(), gg, 1), ALU.mult)
            elif kind == "k":
                e.tensor_tensor(C.ops_k[:, li, :], C.kfm[:, b2, :], C.esc[:, :],
                                ALU.mult)
            elif kind == "r":
                e.tensor_tensor(C.ops_r[:, li, :], C.rfm[:, b2, :], C.esc[:, :],
                                ALU.mult)
            elif kind == "mk":
                e.tensor_tensor(C.ops_m[:, 0, :], C.kfm[:, b2, :], C.esc[:, :],
                                ALU.mult)
            else:
                e.tensor_tensor(C.ops_m[:, 1, :], C.rfm[:, b2, :], C.esc[:, :],
                                ALU.mult)
        e.tensor_tensor(C.ruk[:, :], C.rfm[:, b2, :], C.kfm[:, b2, :], ALU.mult) \
            .then_inc(C.s_vc.h, 1)
        e.wait_ge(C.s_vc.h, vceb + 15)
        last = None
        for sl in range(4):
            last = e.tensor_scalar(C.ruk[:, 128 * sl:128 * sl + 128],
                                   C.ruk[:, 128 * sl:128 * sl + 128],
                                   C.c_u[:, sl:sl + 1], None, ALU.mult)
        last.then_inc(C.s_eb.h, 1)
    on("vector", ebuild)
    C.s_eb.inc(NEX + 1)

    def eexp(e, vas0=vas0, veb0=veb0, esc_views=esc_views):
        for j in range(NEX):
            e.wait_ge(C.s_eb.h, veb0 + j + 1)
            e.activation(esc_views[j], esc_views[j], AF.Exp).then_inc(C.s_as.h, 1)
    on("scalar", eexp)
    C.s_as.inc(NEX)

    _scan(C, f, b2)

    _ogn(C, f, b2)


def _scan(C, f, b2):
    on = C.on
    z = C.c_zero1
    veb = C.s_eb.n
    vse0 = C.s_se.n
    vsm0 = C.s_sm.n
    vst_prev = C.s_st.n
    voe_prev = C.s_oe.n

    def mms(e, f=f, b2=b2, veb=veb, vse0=vse0, vsm0=vsm0, vst_prev=vst_prev,
            voe_prev=voe_prev):
        e.wait_ge(C.s_eb.h, veb)
        if f >= 1:
            e.wait_ge(C.s_st.h, vst_prev)
            e.wait_ge(C.s_oe.h, voe_prev)
        for h in range(8):
            sl = h // 2
            p0 = (h % 2) * 64
            col = 128 * sl
            kk, rr, mm_ = C.ops_k.ap(), C.ops_r.ap(), C.ops_m.ap()
            if h >= 1:
                e.wait_ge(C.s_se.h, vse0 + h)
            e.matmul(C.psP[:, 0:128], z[0:1, 0:128], z[0:1, 0:128],
                     start=True, stop=True)
            e.matmul(C.psP[0:64, 64:128], kk[p0:p0 + 64, 4, col:col + 64],
                     rr[p0:p0 + 64, 4, col + 64:col + 128], start=False, stop=True)
            e.matmul(C.psP[:, 128:256], mm_[p0:p0 + 64, 0, col:col + 128],
                     mm_[p0:p0 + 64, 1, col:col + 128], start=False, stop=True)
            e.matmul(C.psP[:, 256:384], kk[p0:p0 + 64, 0, col:col + 128],
                     rr[p0:p0 + 64, 0, col:col + 128], start=False, stop=True)
            e.matmul(C.psP[:, 384:512], kk[p0:p0 + 64, 1, col:col + 128],
                     rr[p0:p0 + 64, 1, col:col + 128], start=False, stop=True)
            e.matmul(C.psQ[:, 0:128], kk[p0:p0 + 64, 2, col:col + 128],
                     rr[p0:p0 + 64, 2, col:col + 128], start=True, stop=True)
            e.matmul(C.psQ[:, 128:256], kk[p0:p0 + 64, 3, col:col + 128],
                     rr[p0:p0 + 64, 3, col:col + 128], start=False, stop=True)
            e.transpose(C.psU[:, 0:64], kk[p0:p0 + 64, 5, col:col + 128],
                        C.c_id_b[p0:p0 + 64, p0:p0 + 64]).then_inc(C.s_sm.h, 1)
            e.wait_ge(C.s_se.h, vse0 + h + 1)
            e.matmul(C.psO[:, 64 * h:64 * h + 64],
                     C.ops_r[p0:p0 + 64, 5, col:col + 128],
                     C.s16[p0:p0 + 64, sl, :],
                     start=(h == 0), stop=False)
            for pi in range(6):
                e.matmul(C.psO[:, 64 * h:64 * h + 64],
                         C.ppiece[:, pi, :], C.vtok[:, b2, 64 * h:64 * h + 64],
                         start=False, stop=(pi == 5))
            e.matmul(C.psB[0:64, 64 * h:64 * h + 64],
                     C.k128tok[:, 0:64], C.vtok[:, b2, 64 * h:64 * h + 64],
                     start=(h == 0), stop=True).then_inc(C.s_sm.h, 1)
        e.wait_ge(C.s_se.h, vse0 + 8)
        last = None
        for sl in range(4):
            last = e.matmul(C.psP[0:2, 128 * sl:128 * sl + 128],
                            C.c_sel2[:, :], C.ruk[:, 128 * sl:128 * sl + 128],
                            start=(sl == 0), stop=True)
        last.then_inc(C.s_sm.h, 1)
    on("tensor", mms)
    C.s_sm.inc(17)

    def sevac(e, vsm0=vsm0, vse0=vse0):
        for h in range(8):
            e.wait_ge(C.s_sm.h, vsm0 + 2 * h + 1)
            e.tensor_copy(C.ppiece[:, 0, :], C.psP[:, 0:128])
            e.tensor_tensor(C.ppiece[:, 5, :], C.psP[:, 128:256],
                            C.c_masks[:, 4, :], ALU.mult)
            e.tensor_tensor(C.ppiece[:, 4, :], C.psP[:, 256:384],
                            C.c_masks[:, 0, :], ALU.mult)
            e.tensor_tensor(C.ppiece[:, 3, :], C.psP[:, 384:512],
                            C.c_masks[:, 1, :], ALU.mult)
            e.tensor_tensor(C.ppiece[:, 2, :], C.psQ[:, 0:128],
                            C.c_masks[:, 2, :], ALU.mult)
            e.tensor_tensor(C.ppiece[:, 1, :], C.psQ[:, 128:256],
                            C.c_masks[:, 3, :], ALU.mult)
            e.tensor_copy(C.k128tok[:, 0:64], C.psU[:, 0:64]) \
                .then_inc(C.s_se.h, 1)
    on("vector", sevac)
    C.s_se.inc(8)

    vsm_all = C.s_sm.n

    def acols(e, vce=C.s_ce.n):
        e.wait_ge(C.s_ce.h, vce)
        chv = C.chat.ap().rearrange("p (a u) -> p a u", a=4)
        e.activation(C.acol[:, 0:4],
                     chv[:, :, 127:128].rearrange("p a u -> p (a u)"),
                     AF.Exp, scale=-1.0).then_inc(C.s_as.h, 1)
    on("scalar", acols)
    vac = C.s_as.inc()

    vds0 = C.s_ds.n
    def dsev(e, vsm_all=vsm_all, vds0=vds0):
        e.wait_ge(C.s_sm.h, vsm_all)
        e.tensor_copy(C.dstmp[0:64, :], C.psB[0:64, :]).then_inc(C.s_ds.h, 1)
    on("vector", dsev)
    C.s_ds.inc()

    def dsshift(e, vds0=vds0):
        e.wait_ge(C.s_ds.h, vds0 + 1)
        dv = C.dstmp.ap().rearrange("p (a c u) -> p a c u", a=4, c=2)
        e.dma_start(out=dv[64:128, :, 1, :], in_=dv[0:64, :, 1, :]) \
            .then_inc(C.s_ds.h, 16)
    on("sync", dsshift)
    C.s_ds.inc(16)

    vccb = C.s_vc.n
    C.s_vc.inc(1)
    def comb(e, vac=vac, vccb=vccb, vds=C.s_ds.n):
        e.wait_ge(C.s_ds.h, vds)
        e.wait_ge(C.s_as.h, vac)
        last = None
        for sl in range(4):
            last = e.scalar_tensor_tensor(
                C.s_allt[0:64, sl, :], C.s_allt[0:64, sl, :],
                C.acol[0:64, sl:sl + 1],
                C.dstmp[0:64, 128 * sl:128 * sl + 64], ALU.mult, ALU.add)
            last = e.scalar_tensor_tensor(
                C.s_allt[64:128, sl, :], C.s_allt[64:128, sl, :],
                C.acol[64:128, sl:sl + 1],
                C.dstmp[64:128, 128 * sl + 64:128 * sl + 128], ALU.mult, ALU.add)
        last.then_inc(C.s_vc.h, 1)
        e.wait_ge(C.s_vc.h, vccb + 1)
        e.tensor_copy(C.s16.ap(), C.s_allt.ap()).then_inc(C.s_st.h, 1)
    on("vector", comb)
    C.s_st.inc()


def _ogn(C, f, b2):
    on = C.on
    t = C.tiny
    vsm_all = C.s_sm.n

    def cuev(e, vsm_all=vsm_all):
        e.wait_ge(C.s_sm.h, vsm_all)
        e.tensor_copy(C.cu_sb[0:2, :], C.psP[0:2, 0:512]).then_inc(C.s_se.h, 1)
    on("vector", cuev)
    vcue = C.s_se.inc()

    def cuT(e, vcue=vcue):
        e.wait_ge(C.s_se.h, vcue)
        last = None
        for sl in range(4):
            last = e.transpose(C.psU[:, 192 + 2 * sl:194 + 2 * sl],
                               C.cu_sb[0:2, 128 * sl:128 * sl + 128],
                               C.c_id_b[0:2, 0:2])
        last.then_inc(C.s_sm.h, 1)
    on("tensor", cuT)
    vcuT = C.s_sm.inc()

    def cuTe(e, vcuT=vcuT):
        e.wait_ge(C.s_sm.h, vcuT)
        e.tensor_copy(C.cuT[:, 0:8], C.psU[:, 192:200]).then_inc(C.s_se.h, 1)
    on("vector", cuTe)
    C.s_se.inc()

    vse_all = C.s_se.n

    vcg = C.s_vc.n
    C.s_vc.inc(7)
    def ognv(e, b2=b2, vse_all=vse_all, vcg=vcg):
        e.wait_ge(C.s_se.h, vse_all)
        last = None
        for h in range(8):
            cidx = 2 * (h // 2) + (h % 2)
            last = e.scalar_tensor_tensor(C.uterm[:, 64 * h:64 * h + 64],
                                          C.vtok[:, b2, 64 * h:64 * h + 64],
                                          C.cuT[:, cidx:cidx + 1],
                                          C.uterm[:, 64 * h:64 * h + 64],
                                          ALU.mult, ALU.bypass)
        last.then_inc(C.s_vc.h, 1)
        e.wait_ge(C.s_vc.h, vcg + 1)
        e.tensor_tensor(C.o_sb[:, :], C.psO[:, :], C.uterm[:, :], ALU.add) \
            .then_inc(C.s_vc.h, 1)
        e.wait_ge(C.s_vc.h, vcg + 2)
        e.tensor_tensor(C.osq[:, :], C.o_sb[:, :], C.o_sb[:, :], ALU.mult) \
            .then_inc(C.s_vc.h, 1)
        e.wait_ge(C.s_vc.h, vcg + 3)
        ov = C.o_sb.ap().rearrange("p (g u) -> p g u", g=8)
        qv = C.osq.ap().rearrange("p (g u) -> p g u", g=8)
        e.tensor_reduce(C.gstat[:, 0:8], ov, axis=AXX, op=ALU.add)
        e.tensor_reduce(C.gstat[:, 8:16], qv, axis=AXX, op=ALU.add) \
            .then_inc(C.s_vc.h, 1)
        e.wait_ge(C.s_vc.h, vcg + 4)
        e.tensor_scalar(C.gstat[:, 0:8], C.gstat[:, 0:8], 1.0 / 64, None,
                        ALU.mult)
        e.tensor_scalar(C.gstat[:, 8:16], C.gstat[:, 8:16], 1.0 / 64, None,
                        ALU.mult).then_inc(C.s_vc.h, 1)
        e.wait_ge(C.s_vc.h, vcg + 5)
        e.tensor_tensor(C.gstat[:, 16:24], C.gstat[:, 0:8], C.gstat[:, 0:8],
                        ALU.mult).then_inc(C.s_vc.h, 1)
        e.wait_ge(C.s_vc.h, vcg + 6)
        e.tensor_tensor(C.gstat[:, 8:16], C.gstat[:, 8:16], C.gstat[:, 16:24],
                        ALU.subtract).then_inc(C.s_vc.h, 1)
        e.wait_ge(C.s_vc.h, vcg + 7)
        e.tensor_scalar(C.gstat[:, 8:16], C.gstat[:, 8:16], 1e-5, None,
                        ALU.add).then_inc(C.s_oe.h, 1)
    on("vector", ognv)
    voe1 = C.s_oe.inc()

    scg = C.s_sc.n
    C.s_sc.inc(1)
    def ogns(e, voe1=voe1, scg=scg):
        e.wait_ge(C.s_oe.h, voe1)
        e.activation(C.gstat[:, 8:16], C.gstat[:, 8:16], AF.Ln) \
            .then_inc(C.s_sc.h, 1)
        e.wait_ge(C.s_sc.h, scg + 1)
        e.activation(C.gstat[:, 8:16], C.gstat[:, 8:16], AF.Exp, scale=-0.5) \
            .then_inc(C.s_as.h, 1)
    on("scalar", ogns)
    vgn = C.s_as.inc()

    vcg2 = C.s_vc.n
    C.s_vc.inc(4)
    def ognv2(e, vgn=vgn, b2=b2, vcg2=vcg2):
        e.wait_ge(C.s_as.h, vgn)
        ov = C.o_sb.ap().rearrange("p (g u) -> p g u", g=8)
        e.tensor_tensor(ov, ov, C.gstat[:, 0:8].rearrange("p (g u) -> p g u", u=1)
                        .broadcast_to([128, 8, 64]), ALU.subtract) \
            .then_inc(C.s_vc.h, 1)
        e.wait_ge(C.s_vc.h, vcg2 + 1)
        e.tensor_tensor(ov, ov, C.gstat[:, 8:16].rearrange("p (g u) -> p g u", u=1)
                        .broadcast_to([128, 8, 64]), ALU.mult) \
            .then_inc(C.s_vc.h, 1)
        e.wait_ge(C.s_vc.h, vcg2 + 2)
        e.tensor_tensor(C.o_sb[:, :], C.o_sb[:, :], C.c_gnw[:, :], ALU.mult) \
            .then_inc(C.s_vc.h, 1)
        e.wait_ge(C.s_vc.h, vcg2 + 3)
        e.tensor_tensor(C.o_sb[:, :], C.o_sb[:, :], C.c_gnb[:, :], ALU.add) \
            .then_inc(C.s_vc.h, 1)
        e.wait_ge(C.s_vc.h, vcg2 + 4)
        e.tensor_tensor(C.ogtok[:, :], C.o_sb[:, :], C.gtok[:, :], ALU.mult) \
            .then_inc(C.s_oe.h, 1)
    on("vector", ognv2)
    voe2 = C.s_oe.inc()

    def ogD(e, f=f, voe2=voe2, vzf=(2 * TF // 128) * 16):
        e.wait_ge(C.s_oe.h, voe2)
        e.wait_ge(C.s_zf.h, vzf)
        e.indirect_dma_start(
            out=C.cc_og_in, out_offset=bass.IndirectOffsetOnAxis(
                ap=C.ogidx[:, f:f + 1], axis=0),
            in_=C.ogtok[:, :], in_offset=None).then_inc(C.s_ogd.h, 16)
    on("gpsimd", ogD)
    C.s_ogd.inc(16)





class Chain:
    def __init__(self, nc, es, name):
        self.sem = es.enter_context(nc.semaphore(name + "_s"))
        self.dsem = es.enter_context(nc.semaphore(name + "_d"))
        self.k = 0
        self.d = 0


class Prog:
    ENGS = ("sync", "scalar", "vector", "tensor", "gpsimd")

    def __init__(self):
        self.nc = bass.Bass("TRN2", target_bir_lowering=False)
        self.es = ExitStack()
        self.q = {e: [] for e in self.ENGS}
        self.main = Chain(self.nc, self.es, "m")
        self.aux = Chain(self.nc, self.es, "a")
        self.outs = {}

    def sb(self, name, shape, dt=F32, scope=None):
        return (scope or self.es).enter_context(
            self.nc.sbuf_tensor(name, list(shape), dt))

    def ps(self, name, shape, dt=F32):
        return self.es.enter_context(self.nc.psum_tensor(name, list(shape), dt))

    def din(self, name, shape, dt=F32):
        return self.nc.dram_tensor(name, list(shape), dt,
                                   kind="ExternalInput").ap()

    def dout(self, name, shape, dt=F32):
        ap = self.nc.dram_tensor(name, list(shape), dt,
                                 kind="ExternalOutput").ap()
        self.outs[name] = ap
        return ap

    def dram(self, name, shape, dt=F32):
        return self.nc.dram_tensor(name, list(shape), dt).ap()

    def step(self, eng, fn, dma=0, chain=None, also=()):
        ch = chain or self.main
        waits = [(ch.sem, ch.k), (ch.dsem, 16 * ch.d)]
        for (och, kind, val) in also:
            waits.append((och.sem, val) if kind == "k" else
                         (och.dsem, 16 * val))
        if dma:
            def wrap(e, fn=fn, waits=tuple(waits), ds=ch.dsem):
                for s, v in waits:
                    if v > 0:
                        e.wait_ge(s, v)
                fn(e, ds)
            ch.d += dma
        else:
            def wrap(e, fn=fn, waits=tuple(waits), ss=ch.sem, eng=eng):
                for s, v in waits:
                    if v > 0:
                        e.wait_ge(s, v)
                last = fn(e)
                if last is None:
                    assert eng != "tensor", "tensor step returned None"
                    last = e.engine_nop()
                last.then_inc(ss, 1)
            ch.k += 1
        self.q[eng].append(wrap)

    def build(self):
        def fin(e, m=self.main, a=self.aux):
            if m.k:
                e.wait_ge(m.sem, m.k)
            if m.d:
                e.wait_ge(m.dsem, 16 * m.d)
            if a.k:
                e.wait_ge(a.sem, a.k)
            if a.d:
                e.wait_ge(a.dsem, 16 * a.d)
        self.q["gpsimd"].append(fin)
        with self.nc.Block() as block:
            for eng in self.ENGS:
                deco = getattr(block, eng)
                for f in self.q[eng]:
                    deco(f)
        self.nc.finalize()
        return self.nc


def _ln_steps(P, tv, xt, xn, sq16, eps, rms):
    if not rms:
        P.step("vector", lambda e: e.tensor_reduce(
            tv[:, 0:1], xt[:, :], axis=AXX, op=ALU.add))
        P.step("vector", lambda e: e.tensor_scalar(
            tv[:, 1:2], tv[:, 0:1], 1.0 / D, None, ALU.mult))
        P.step("vector", lambda e: e.tensor_scalar(
            xn[:, :], xt[:, :], tv[:, 1:2], None, ALU.subtract))
        src = xn
    else:
        src = xt
    P.step("scalar", lambda e, src=src: e.activation(
        sq16[:, :], src[:, :], AF.Square, accum_out=tv[:, 2:3]))
    P.step("vector", lambda e: e.tensor_scalar(
        tv[:, 3:4], tv[:, 2:3], 1.0 / D, eps, ALU.mult, ALU.add))
    P.step("scalar", lambda e: e.activation(tv[:, 4:5], tv[:, 3:4], AF.Ln))
    P.step("scalar", lambda e: e.activation(
        tv[:, 5:6], tv[:, 4:5], AF.Exp, scale=-0.5))
    P.step("vector", lambda e, src=src: e.tensor_scalar(
        xn[:, :], src[:, :], tv[:, 5:6], None, ALU.mult))


def _tp8(P, src16, psT, idb, evac):
    def tp(e):
        last = None
        for sl in range(8):
            last = e.transpose(psT[:, 128 * sl:128 * sl + 128],
                               src16[:, 128 * sl:128 * sl + 128], idb[:, :])
        return last
    P.step("tensor", tp)
    P.step("scalar", evac)


def build_b():
    P = Prog()
    x_in = P.din("x_my", [TH, D])
    og_in = P.din("og_my", [TH, D], BF16)
    Wo_in = P.din("Wo", [D, D])
    ln2_in = P.din("ln2_w_c", [128, 8])
    w1_in = P.din("ffn_w1", [D, 4096])
    w2_in = P.din("ffn_w2", [4096, D])
    wr_in = P.din("moe_Wr", [D, 8])
    br_in = P.din("moe_br_rep", [128, 8])
    W1_in = P.din("moe_W1", [8, D, 4096])
    W2_in = P.din("moe_W2", [8, 4096, D])
    b1_in = P.din("moe_b1_c", [8, 128, 32])
    b2_in = P.din("moe_b2", [8, D])
    id_in = P.din("ident", [128, 128])
    x1_out = P.dout("x1", [TH, D])
    s0_out = P.dout("b_s0", [TH, D])
    g_out = P.dout("b_gates", [128, 128])
    y_out = P.dout("b_ymoe", [128, NCH_H, D])

    xv = x_in.rearrange("(c p) d -> c p d", p=128)
    ogv = og_in.rearrange("(c p) d -> c p d", p=128)
    s0v = s0_out.rearrange("(c p) d -> c p d", p=128)
    x1v = x1_out.rearrange("(c p) d -> c p d", p=128)

    xnfm = P.sb("xnfm", [128, 8, TH], BF16)
    h2fm = P.sb("h2fm", [128, 8, TH], BF16)
    yacc = P.sb("yacc", [128, NCH_H, D])
    hq = P.sb("hq", [128, 8, 512], BF16)
    w1b = P.sb("w1b", [128, 8, D], BF16)
    w2b = P.sb("w2b", [128, 8, D], BF16)
    xt = P.sb("xt", [128, D])
    xn = P.sb("xn", [128, D])
    x16 = P.sb("x16", [128, D], BF16)
    ogt = P.sb("ogt", [128, D], BF16)
    fmA = P.sb("fmA", [128, 8, 128], BF16)
    g0f = P.sb("g0f", [128, D])
    lg = P.sb("lg", [128, 128])
    gates = P.sb("gates", [128, 128])
    Xs = P.sb("Xs", [128, 128])
    et = P.sb("et", [128, 128])
    est = P.sb("est", [128, 10, 16])
    tv = P.sb("tv", [128, 16])
    b1a = P.sb("b1a", [128, 32])
    b2r = P.sb("b2r", [1, D])
    brr = P.sb("brr", [128, 8])
    wrb = P.sb("wrb", [128, 8, 8], BF16)
    ln2c = P.sb("ln2c", [128, 8])
    idb = P.sb("idb", [128, 128], BF16)
    onesf = P.sb("onesf", [1, 128])

    psA = P.ps("psA", [128, 512])
    psB = P.ps("psB", [128, 512])
    psQ = P.ps("psQ", [128, 1024])
    psT = P.ps("psT", [128, 1024], BF16)
    psL = P.ps("psL", [128, 8])

    def c0(e, ds):
        e.dma_start(out=ln2c[:], in_=ln2_in).then_inc(ds, 16)
        e.dma_start(out=brr[:], in_=br_in).then_inc(ds, 16)
    P.step("sync", c0, dma=2)

    P.step("gpsimd", lambda e: e.memset(onesf[:], 1.0))

    def c1(e, ds):
        e.dma_start(out=idb[:], in_=id_in).then_inc(ds, 16)
        e.dma_start(out=wrb.ap(),
                    in_=wr_in.rearrange("(a p) n -> p a n", p=128)) \
            .then_inc(ds, 16)
        e.dma_start(out=w1b.ap(),
                    in_=Wo_in.rearrange("(a p) n -> p a n", p=128)) \
            .then_inc(ds, 16)
    P.step("gpsimd", c1, dma=3)

    for c in range(NCH_H):
        P.step("sync", lambda e, ds, c=c: e.dma_start(
            out=xt[:], in_=xv[c]).then_inc(ds, 16), dma=1)
        _ln_steps(P, tv, xt, xn, x16, 1e-5, rms=False)
        P.step("vector", lambda e: e.tensor_copy(x16[:, :], xn[:, :]))
        _tp8(P, x16, psT, idb, lambda e, c=c: e.activation(
            xnfm[:, :, 128 * c:128 * c + 128],
            psT.ap().rearrange("p (a n) -> p a n", a=8), AF.Copy))
        P.step("sync", lambda e, ds, c=c: e.dma_start(
            out=ogt[:], in_=ogv[c]).then_inc(ds, 16), dma=1)
        _tp8(P, ogt, psT, idb, lambda e: e.activation(
            fmA.ap(), psT.ap().rearrange("p (a n) -> p a n", a=8), AF.Copy))
        def womm(e):
            last = None
            for half in range(2):
                ps = psA if half == 0 else psB
                for ks in range(8):
                    last = e.matmul(ps[:, :], fmA[:, ks, :],
                                    w1b[:, ks, 512 * half:512 * half + 512],
                                    start=(ks == 0), stop=(ks == 7))
            return last
        P.step("tensor", womm)
        def x1a_ev(e):
            e.tensor_tensor(xt[:, 0:512], psA[:, :], xn[:, 0:512], ALU.add)
            return e.tensor_tensor(xt[:, 512:1024], psB[:, :],
                                   xn[:, 512:1024], ALU.add)
        P.step("vector", x1a_ev)
        P.step("sync", lambda e, ds, c=c: e.dma_start(
            out=s0v[c], in_=xt[:]).then_inc(ds, 16), dma=1)
        _ln_steps(P, tv, xt, xn, x16, 1e-6, rms=True)
        P.step("vector", lambda e: e.tensor_copy(x16[:, :], xn[:, :]))
        def h2ev(e, c=c):
            last = None
            for sl in range(8):
                last = e.activation(h2fm[:, sl, 128 * c:128 * c + 128],
                                    psT[:, 128 * sl:128 * sl + 128], AF.Copy,
                                    scale=ln2c[:, sl:sl + 1])
            return last
        _tp8(P, x16, psT, idb, h2ev)
        def lgm(e, c=c):
            last = None
            for ks in range(8):
                last = e.matmul(psL[:, :], xnfm[:, ks, 128 * c:128 * c + 128],
                                wrb[:, ks, :], start=(ks == 0), stop=(ks == 7))
            return last
        P.step("tensor", lgm)
        P.step("vector", lambda e, c=c: e.tensor_tensor(
            lg[:, 8 * c:8 * c + 8], psL[:, :], brr[:, :], ALU.add))

    def ffn_block(get_w1, get_w2, act_fn, b1col, evac_fn, get_h2):
        for tg in range(4):
            for fs in range(8):
                ps = psA if fs % 2 == 0 else psB
                def f1(e, tg=tg, fs=fs, ps=ps):
                    last = None
                    for ks in range(8):
                        last = e.matmul(ps[:, :], get_w1()[:, ks, 128 * fs:128 * fs + 128],
                                        get_h2()[:, ks, 512 * tg:512 * tg + 512],
                                        start=(ks == 0), stop=(ks == 7))
                    return last
                P.step("tensor", f1)
                if b1col is None:
                    P.step("scalar", lambda e, fs=fs, ps=ps: e.activation(
                        hq[:, fs, :], ps[:, :], act_fn))
                else:
                    P.step("scalar", lambda e, fs=fs, ps=ps: e.activation(
                        hq[:, fs, :], ps[:, :], act_fn, bias=b1col(fs)))
            for tt in range(4):
                def f2(e, tg=tg, tt=tt):
                    last = None
                    for half in range(2):
                        for fs in range(8):
                            last = e.matmul(
                                psQ[:, 512 * half:512 * half + 512],
                                hq[:, fs, 128 * tt:128 * tt + 128],
                                get_w2()[:, fs, 512 * half:512 * half + 512],
                                start=(fs == 0), stop=(fs == 7))
                    return last
                P.step("tensor", f2)
                evac_fn(4 * tg + tt)

    for q in range(4):
        def wld(e, ds, q=q):
            e.dma_start(out=w1b.ap(),
                        in_=w1_in[:, 1024 * q:1024 * q + 1024]
                        .rearrange("(a p) n -> p a n", p=128)).then_inc(ds, 16)
            e.dma_start(out=w2b.ap(),
                        in_=w2_in[1024 * q:1024 * q + 1024, :]
                        .rearrange("(a p) n -> p a n", p=128)).then_inc(ds, 16)
        P.step("gpsimd", wld, dma=2)

        def ev_attn(c, q=q):
            if q == 0:
                P.step("vector", lambda e, c=c: e.tensor_copy(
                    yacc[:, c, :], psQ[:, :]))
            else:
                P.step("vector", lambda e, c=c: e.tensor_tensor(
                    yacc[:, c, :], yacc[:, c, :], psQ[:, :], ALU.add))
        ffn_block(lambda: w1b, lambda: w2b, AF.Gelu, None, ev_attn,
                  lambda: h2fm)

    for c in range(NCH_H):
        P.step("sync", lambda e, ds, c=c: e.dma_start(
            out=xt[:], in_=s0v[c]).then_inc(ds, 16), dma=1)
        P.step("vector", lambda e, c=c: e.tensor_tensor(
            yacc[:, c, :], yacc[:, c, :], xt[:, :], ALU.add))
        P.step("sync", lambda e, ds, c=c: e.dma_start(
            out=s0v[c], in_=yacc[:, c, :]).then_inc(ds, 16), dma=1)

    V3 = lambda t: t.ap().rearrange("p (c e) -> p c e", e=8)
    col = lambda t, k: t[:, k, :]
    bc = lambda t, k: t.ap()[:, k, :].rearrange("p (c u) -> p c u", u=1) \
        .broadcast_to([128, 16, 8])
    MX, TLO, THI, DM, TM, FLO, FM, PR, MK, GS = range(10)
    P.step("vector", lambda e: e.tensor_scalar(
        Xs[:, :], lg[:, :], 0.5, None, ALU.mult))
    P.step("vector", lambda e: e.tensor_reduce(
        col(est, MX), V3(Xs), axis=AXX, op=ALU.max))
    P.step("vector", lambda e: e.tensor_scalar(
        col(est, TLO), col(est, MX), -1.0, None, ALU.add))
    P.step("vector", lambda e: e.tensor_scalar(
        col(est, THI), col(est, MX), -(0.125 ** 0.5), None, ALU.add))
    P.step("vector", lambda e: e.tensor_tensor(
        col(est, DM), col(est, THI), col(est, TLO), ALU.subtract))
    P.step("vector", lambda e: e.tensor_tensor(
        V3(et), V3(Xs), bc(est, TLO), ALU.subtract))
    P.step("vector", lambda e: e.tensor_scalar(
        et[:, :], et[:, :], 0.0, None, ALU.max))
    P.step("vector", lambda e: e.tensor_tensor(
        et[:, :], et[:, :], et[:, :], ALU.mult))
    P.step("vector", lambda e: e.tensor_reduce(
        col(est, FLO), V3(et), axis=AXX, op=ALU.add))
    P.step("vector", lambda e: e.tensor_scalar(
        col(est, FLO), col(est, FLO), -1.0, None, ALU.add))
    for it in range(25):
        P.step("vector", lambda e: e.tensor_scalar(
            col(est, DM), col(est, DM), 0.5, None, ALU.mult))
        P.step("vector", lambda e: e.tensor_tensor(
            col(est, TM), col(est, TLO), col(est, DM), ALU.add))
        P.step("vector", lambda e: e.tensor_tensor(
            V3(et), V3(Xs), bc(est, TM), ALU.subtract))
        P.step("vector", lambda e: e.tensor_scalar(
            et[:, :], et[:, :], 0.0, None, ALU.max))
        P.step("vector", lambda e: e.tensor_tensor(
            et[:, :], et[:, :], et[:, :], ALU.mult))
        P.step("vector", lambda e: e.tensor_reduce(
            col(est, FM), V3(et), axis=AXX, op=ALU.add))
        P.step("vector", lambda e: e.tensor_scalar(
            col(est, FM), col(est, FM), -1.0, None, ALU.add))
        P.step("vector", lambda e: e.tensor_tensor(
            col(est, PR), col(est, FM), col(est, FLO), ALU.mult))
        P.step("vector", lambda e: e.tensor_scalar(
            col(est, MK), col(est, PR), 0.0, None, ALU.is_ge))
        P.step("vector", lambda e: e.tensor_tensor(
            col(est, PR), col(est, TM), col(est, TLO), ALU.subtract))
        P.step("vector", lambda e: e.tensor_tensor(
            col(est, PR), col(est, PR), col(est, MK), ALU.mult))
        P.step("vector", lambda e: e.tensor_tensor(
            col(est, TLO), col(est, TLO), col(est, PR), ALU.add))
    P.step("vector", lambda e: e.tensor_tensor(
        V3(et), V3(Xs), bc(est, TM), ALU.subtract))
    P.step("vector", lambda e: e.tensor_scalar(
        et[:, :], et[:, :], 0.0, None, ALU.max))
    P.step("vector", lambda e: e.tensor_tensor(
        et[:, :], et[:, :], et[:, :], ALU.mult))
    P.step("vector", lambda e: e.tensor_reduce(
        col(est, GS), V3(et), axis=AXX, op=ALU.add))
    P.step("vector", lambda e: e.reciprocal(col(est, MK), col(est, GS)))
    P.step("vector", lambda e: e.tensor_tensor(
        V3(gates), V3(et), bc(est, MK), ALU.mult))
    P.step("vector", lambda e: e.tensor_scalar(
        et[:, :], gates[:, :], 1e-9, None, ALU.is_gt))
    P.step("vector", lambda e: e.tensor_tensor(
        gates[:, :], gates[:, :], et[:, :], ALU.mult))
    P.step("sync", lambda e, ds: e.dma_start(
        out=g_out, in_=gates[:]).then_inc(ds, 16), dma=1)

    for ex in range(8):
        for q in range(4):
            def wld(e, ds, ex=ex, q=q):
                e.dma_start(out=w1b.ap(),
                            in_=W1_in[ex][:, 1024 * q:1024 * q + 1024]
                            .rearrange("(a p) n -> p a n", p=128)) \
                    .then_inc(ds, 16)
                e.dma_start(out=w2b.ap(),
                            in_=W2_in[ex][1024 * q:1024 * q + 1024, :]
                            .rearrange("(a p) n -> p a n", p=128)) \
                    .then_inc(ds, 16)
                if q == 0:
                    e.dma_start(out=b1a[:], in_=b1_in[ex]).then_inc(ds, 16)
                    e.dma_start(out=b2r[:], in_=b2_in[ex:ex + 1, :]) \
                        .then_inc(ds, 16)
            P.step("gpsimd", wld, dma=(4 if q == 0 else 2))

            def ev_moe(c, ex=ex, q=q):
                gcol = gates[:, 8 * c + ex:8 * c + ex + 1]
                if ex == 0 and q == 0:
                    P.step("vector", lambda e, c=c, gcol=gcol: e.tensor_scalar(
                        yacc[:, c, :], psQ[:, :], gcol, None, ALU.mult))
                else:
                    P.step("vector",
                           lambda e, c=c, gcol=gcol: e.scalar_tensor_tensor(
                               yacc[:, c, :], psQ[:, :], gcol, yacc[:, c, :],
                               ALU.mult, ALU.add))
            ffn_block(lambda: w1b, lambda: w2b, AF.Relu,
                      lambda fs, q=q: b1a[:, 8 * q + fs:8 * q + fs + 1],
                      ev_moe, lambda: xnfm)
        def b2rep2(e):
            e.matmul(psQ[:, 0:512], onesf[0:1, :], b2r[0:1, 0:512],
                     start=True, stop=True)
            return e.matmul(psQ[:, 512:1024], onesf[0:1, :],
                            b2r[0:1, 512:1024], start=True, stop=True)
        P.step("tensor", b2rep2)
        for c in range(NCH_H):
            gcol = gates[:, 8 * c + ex:8 * c + ex + 1]
            P.step("vector", lambda e, c=c, gcol=gcol: e.scalar_tensor_tensor(
                yacc[:, c, :], psQ[:, :], gcol, yacc[:, c, :],
                ALU.mult, ALU.add))
    P.step("sync", lambda e, ds: e.dma_start(
        out=y_out, in_=yacc.ap()).then_inc(ds, 16), dma=1)

    for c in range(NCH_H):
        def cld(e, ds, c=c):
            e.dma_start(out=xt[:], in_=xv[c]).then_inc(ds, 16)
            e.dma_start(out=xn[:], in_=s0v[c]).then_inc(ds, 16)
        P.step("sync", cld, dma=2)
        P.step("scalar", lambda e, c=c: e.activation(
            g0f[:, :], yacc[:, c, :], AF.Silu))
        P.step("vector", lambda e: e.tensor_tensor(
            xn[:, :], xn[:, :], g0f[:, :], ALU.mult))
        P.step("vector", lambda e: e.tensor_tensor(
            xn[:, :], xn[:, :], xt[:, :], ALU.add))
        P.step("sync", lambda e, ds, c=c: e.dma_start(
            out=x1v[c], in_=xn[:]).then_inc(ds, 16), dma=1)

    return P


def _prep_b(I):
    common = dict(
        Wo=I["L0_attn_Wo"],
        ln2_w_c=np.ascontiguousarray(I["L0_attn_ln2_w"].reshape(8, 128).T),
        ffn_w1=I["L0_attn_ffn_w1"], ffn_w2=I["L0_attn_ffn_w2"],
        moe_Wr=I["L0_ffn_Wr"],
        moe_br_rep=np.ascontiguousarray(
            np.broadcast_to(I["L0_ffn_br"], (128, 8))),
        moe_W1=I["L0_ffn_W1"], moe_W2=I["L0_ffn_W2"],
        moe_b1_c=np.ascontiguousarray(
            I["L0_ffn_b1"].reshape(8, 32, 128).transpose(0, 2, 1)),
        moe_b2=I["L0_ffn_b2"],
        ident=np.eye(128, dtype=np.float32),
    )
    maps = []
    for c in range(8):
        b, th = c // 2, c % 2
        m = dict(common)
        m["x_my"] = np.ascontiguousarray(I["L0_attn_x"][b, th * TH:(th + 1) * TH])
        maps.append(m)
    return maps

def build_c():
    P = Prog()
    x_in = P.din("x1_full", [TF, D])
    wq_in = P.din("WQ", [D, D])
    wk_in = P.din("WK", [D, D])
    wv_in = P.din("WV", [D, D])
    qc_in = P.din("qcos", [TH, D])
    qs_in = P.din("qsin", [TH, D])
    kc_in = P.din("kcos", [TF, D])
    ks_in = P.din("ksin", [TF, D])
    dec_in = P.din("dec", [TH, TF])
    id_in = P.din("ident", [128, 128])
    s1_out = P.dout("s1", [TH, D])

    xv = x_in.rearrange("(c p) d -> c p d", p=128)
    qcv = qc_in.rearrange("(c p) d -> c p d", p=128)
    qsv = qs_in.rearrange("(c p) d -> c p d", p=128)
    kcv = kc_in.rearrange("(c p) d -> c p d", p=128)
    ksv = ks_in.rearrange("(c p) d -> c p d", p=128)
    decv = dec_in.rearrange("(c p) j -> c p j", p=128)
    s1v = s1_out.rearrange("(c p) d -> c p d", p=128)

    v_hbm = P.dram("v_hbm", [TF, D], BF16)
    q_hbm = P.dram("q_hbm", [TH, D], BF16)
    vhv = v_hbm.rearrange("(c p) d -> c p d", p=128)
    qhv = q_hbm.rearrange("(c p) d -> c p d", p=128)

    xn1fm = P.sb("xn1fm", [128, 8, TF], BF16)
    kfm = P.sb("kfm", [128, 8, TF], BF16)
    scb = P.sb("scb", [128, TF])
    bs = P.sb("bs", [128, TF])
    pd16 = P.sb("pd16", [128, TF], BF16)
    PT = P.sb("PT", [128, 32, 128], BF16)
    wb = P.sb("wb", [128, 8, D], BF16)
    xt = P.sb("xt", [128, D])
    xn = P.sb("xn", [128, D])
    x16 = P.sb("x16", [128, D], BF16)
    qfm = P.sb("qfm", [128, 8, 128], BF16)
    tv = P.sb("tv", [128, 16])
    idb = P.sb("idb", [128, 128], BF16)

    psS = P.ps("psS", [128, 512])
    psO = P.ps("psO", [128, 1024])
    psT = P.ps("psT", [128, 1024], BF16)

    vvw = xn1fm.ap().rearrange("p a (c n) -> p (a c) n", c=4)

    def wld(e, ds, src=None):
        e.dma_start(out=wb.ap(),
                    in_=src.rearrange("(a p) n -> p a n", p=128)) \
            .then_inc(ds, 16)

    P.step("gpsimd", lambda e, ds: e.dma_start(
        out=idb[:], in_=id_in).then_inc(ds, 16), dma=1)
    P.step("gpsimd", lambda e, ds: wld(e, ds, wk_in), dma=1)

    def proj(e, c, half=None):
        last = None
        for h in ((0, 1) if half is None else (half,)):
            for ks in range(8):
                last = e.matmul(psO[:, 512 * h:512 * h + 512],
                                xn1fm[:, ks, 128 * c:128 * c + 128],
                                wb[:, ks, 512 * h:512 * h + 512],
                                start=(ks == 0), stop=(ks == 7))
        return last

    def xpos(dst, cos_ap, sin_ap):
        dv = dst.ap().rearrange("p (n two) -> p n two", two=2)
        cv = cos_ap.rearrange("p (n two) -> p n two", two=2)
        sv = sin_ap.rearrange("p (n two) -> p n two", two=2)
        kxv = scb.ap()[:, 0:D].rearrange("p (n two) -> p n two", two=2)
        E = lambda a: a[:, :, 0:1]
        O = lambda a: a[:, :, 1:2]
        P.step("vector", lambda e: e.tensor_tensor(
            E(kxv), O(dv), E(sv), ALU.mult))
        P.step("vector", lambda e: e.tensor_tensor(
            O(kxv), E(dv), O(sv), ALU.mult))
        P.step("vector", lambda e: e.tensor_tensor(
            E(dv), E(dv), E(cv), ALU.mult))
        P.step("vector", lambda e: e.tensor_tensor(
            O(dv), O(dv), O(cv), ALU.mult))
        P.step("vector", lambda e: e.tensor_tensor(
            E(dv), E(dv), E(kxv), ALU.subtract))
        P.step("vector", lambda e: e.tensor_tensor(
            O(dv), O(dv), O(kxv), ALU.add))

    kc = bs.ap()[:, 0:D]
    ks_ = bs.ap()[:, D:2 * D]

    for c in range(NCH_F):
        P.step("sync", lambda e, ds, c=c: e.dma_start(
            out=xt[:], in_=xv[c]).then_inc(ds, 16), dma=1)
        _ln_steps(P, tv, xt, xn, x16, 1e-5, rms=False)
        P.step("vector", lambda e: e.tensor_copy(x16[:, :], xn[:, :]))
        _tp8(P, x16, psT, idb, lambda e, c=c: e.activation(
            xn1fm[:, :, 128 * c:128 * c + 128],
            psT.ap().rearrange("p (a n) -> p a n", a=8), AF.Copy))
        P.step("tensor", lambda e, c=c: proj(e, c))
        P.step("scalar", lambda e: e.activation(xn.ap(), psO.ap(), AF.Copy))
        def trig(e, ds, c=c):
            e.dma_start(out=kc, in_=kcv[c]).then_inc(ds, 16)
            e.dma_start(out=ks_, in_=ksv[c]).then_inc(ds, 16)
        P.step("sync", trig, dma=2)
        xpos(xn, bs.ap()[:, 0:D], bs.ap()[:, D:2 * D])
        P.step("vector", lambda e: e.tensor_copy(x16[:, :], xn[:, :]))
        _tp8(P, x16, psT, idb, lambda e, c=c: e.activation(
            kfm[:, :, 128 * c:128 * c + 128],
            psT.ap().rearrange("p (a n) -> p a n", a=8), AF.Copy))

    P.step("gpsimd", lambda e, ds: wld(e, ds, wv_in), dma=1)
    for c in range(NCH_F):
        P.step("tensor", lambda e, c=c: proj(e, c))
        P.step("scalar", lambda e: e.activation(x16.ap(), psO.ap(), AF.Copy))
        P.step("sync", lambda e, ds, c=c: e.dma_start(
            out=vhv[c], in_=x16[:]).then_inc(ds, 16), dma=1)

    P.step("gpsimd", lambda e, ds: wld(e, ds, wq_in), dma=1)
    for c in range(NCH_H):
        P.step("tensor", lambda e, c=c: proj(e, c))
        P.step("scalar", lambda e: e.activation(xn.ap(), psO.ap(), AF.Copy))
        def trig(e, ds, c=c):
            e.dma_start(out=kc, in_=qcv[c]).then_inc(ds, 16)
            e.dma_start(out=ks_, in_=qsv[c]).then_inc(ds, 16)
        P.step("sync", trig, dma=2)
        xpos(xn, bs.ap()[:, 0:D], bs.ap()[:, D:2 * D])
        P.step("vector", lambda e: e.tensor_copy(x16[:, :], xn[:, :]))
        P.step("sync", lambda e, ds, c=c: e.dma_start(
            out=qhv[c], in_=x16[:]).then_inc(ds, 16), dma=1)

    for mt in range(NCH_F):
        P.step("sync", lambda e, ds, mt=mt: e.dma_start(
            out=vvw[:, mt, :], in_=vhv[mt]).then_inc(ds, 16), dma=1)

    for qt in range(NCH_H):
        def qld(e, ds, qt=qt):
            e.dma_start(out=x16[:], in_=qhv[qt]).then_inc(ds, 16)
            e.dma_start(out=bs.ap(), in_=decv[qt]).then_inc(ds, 16)
        P.step("sync", qld, dma=2)
        _tp8(P, x16, psT, idb, lambda e: e.activation(
            qfm.ap(), psT.ap().rearrange("p (a n) -> p a n", a=8), AF.Copy))
        for mg in range(8):
            def smm(e, mg=mg):
                last = None
                for ks in range(8):
                    last = e.matmul(psS[:, :], qfm[:, ks, :],
                                    kfm[:, ks, 512 * mg:512 * mg + 512],
                                    start=(ks == 0), stop=(ks == 7))
                return last
            P.step("tensor", smm)
            P.step("scalar", lambda e, mg=mg: e.activation(
                scb[:, 512 * mg:512 * mg + 512], psS[:, :], AF.Copy))
        P.step("vector", lambda e: e.tensor_reduce(
            tv[:, 0:1], scb[:, :], axis=AXX, op=ALU.max))
        P.step("vector", lambda e: e.tensor_scalar(
            tv[:, 1:2], tv[:, 0:1], -1.0, None, ALU.mult))
        P.step("scalar", lambda e: e.activation(
            scb[:, :], scb[:, :], AF.Exp, bias=tv[:, 1:2],
            accum_out=tv[:, 2:3]))
        P.step("vector", lambda e: e.reciprocal(tv[:, 3:4], tv[:, 2:3]))
        P.step("vector", lambda e: e.scalar_tensor_tensor(
            pd16[:, :], scb[:, :], tv[:, 3:4], bs[:, :], ALU.mult, ALU.mult))
        for r in range(4):
            _tp8(P, pd16.ap()[:, 1024 * r:1024 * r + 1024], psT, idb,
                 lambda e, r=r: e.activation(
                     PT[:, 8 * r:8 * r + 8, :],
                     psT.ap().rearrange("p (a n) -> p a n", a=8), AF.Copy))
        def av(e):
            last = None
            for mt in range(32):
                for h in range(2):
                    last = e.matmul(psO[:, 512 * h:512 * h + 512],
                                    PT[:, mt, :], vvw[:, mt, 512 * h:512 * h + 512],
                                    start=(mt == 0), stop=(mt == 31))
            return last
        P.step("tensor", av)
        P.step("scalar", lambda e: e.activation(xt.ap(), psO.ap(), AF.Copy))
        P.step("sync", lambda e, ds, qt=qt: e.dma_start(
            out=s1v[qt], in_=xt[:]).then_inc(ds, 16), dma=1)

    return P


def _xpos_tables(t0):
    half = D // 2
    scale_vec = (np.arange(0, D, 2) + 0.4 * D) / (1.4 * D)
    inv_freq = 1.0 / 10000.0 ** (np.arange(half) / half)

    def tab(pos, down):
        power = pos / 512.0
        scale = scale_vec[None, :] ** power[:, None]
        if down:
            scale = 1.0 / scale
        sinus = pos[:, None] * inv_freq[None, :]
        cos = np.repeat(np.cos(sinus) * scale, 2, axis=-1)
        sin = np.repeat(np.sin(sinus) * scale, 2, axis=-1)
        return (np.ascontiguousarray(cos.astype(np.float32)),
                np.ascontiguousarray(sin.astype(np.float32)))
    qpos = (t0 + np.arange(TH)).astype(np.float64)
    kpos = ((np.arange(TF) + t0) % TF).astype(np.float64)
    qc, qs = tab(qpos, False)
    kc, ks = tab(kpos, True)
    dec = (0.9 ** np.abs(qpos[:, None] - kpos[None, :])).astype(np.float32)
    return qc, qs, kc, ks, np.ascontiguousarray(dec)


def _prep_c(I, x1_full):
    common = dict(WQ=I["L1_attn_W_Q"], WK=I["L1_attn_W_K"],
                  WV=I["L1_attn_W_V"], ident=np.eye(128, dtype=np.float32))
    maps = []
    for c in range(8):
        b, th = c // 2, c % 2
        t0 = th * TH
        qc, qs, kc, ks, dec = _xpos_tables(t0)
        m = dict(common)
        m["x1_full"] = np.ascontiguousarray(
            np.roll(x1_full[b], -t0, axis=0))
        m["qcos"], m["qsin"], m["kcos"], m["ksin"] = qc, qs, kc, ks
        m["dec"] = dec
        maps.append(m)
    return maps


def build_d(lim=None):
    P = Prog()
    x_in = P.din("x1_my", [TH, D])
    s_in = P.din("s1_my", [TH, D])
    wq_in = P.din("w_q", [D, 2048])
    gr_in = P.din("bn_gamma_rep", [128, 2048])
    br_in = P.din("bn_beta_rep", [128, 2048])
    keys_in = P.din("keys_r", [16, 128, 256])
    wd_in = P.din("down_embed", [65536, D])
    wu_in = P.din("up_embed", [65536, D])
    sel_in = P.din("sel64", [128, 64, 128])
    iota_in = P.din("iota64", [128, 64])
    id_in = P.din("ident", [128, 128])
    out = P.dout("xout", [TH, D])
    dst_out = P.dout("d_stats", [2, 2048])
    didx_out = P.dout("d_idx", [128, NCH_H, 64])
    dg1_out = P.dout("d_g1", [TH, D], BF16)

    x1v = x_in.rearrange("(c p) d -> c p d", p=128)
    s1v = s_in.rearrange("(c p) d -> c p d", p=128)
    outv = out.rearrange("(c p) d -> c p d", p=128)
    dg1v = dg1_out.rearrange("(c p) d -> c p d", p=128)

    cc_in = P.dram("cc_bn_in", [2, 2048])
    cc_out = P.dram("cc_bn_out", [2, 2048])

    xn1tok = P.sb("xn1tok", [128, NCH_H, D], BF16)
    qq = P.sb("qq", [128, NCH_H, 2048], BF16)
    keysb = P.sb("keysb", [128, 16, 256])
    ssum = P.sb("ssum", [128, 2048])
    ssq = P.sb("ssq", [128, 2048])
    qsq = P.sb("qsq", [128, 512])
    xt = P.sb("xt", [128, D])
    xn = P.sb("xn", [128, D])
    sq16t = P.sb("sq16t", [128, D], BF16)
    tv = P.sb("tv", [128, 16])
    idb = P.sb("idb", [128, 128], BF16)
    idf = P.sb("idf", [128, 128])
    onesf = P.sb("onesf", [128, 128])
    onesb = P.sb("onesb", [128, 128], BF16)

    s1s = ExitStack()
    wqb = P.sb("wqb", [128, 8, 2048], BF16, scope=s1s)
    xn1fm = P.sb("xn1fm", [128, 8, 2048], BF16, scope=s1s)

    psS = P.ps("psS", [128, 512])
    psU = P.ps("psU", [128, 512])
    psV = P.ps("psV", [128, 512])
    psT = P.ps("psT", [128, 1024], BF16)
    psQd = P.ps("psQd", [128, 1024])
    psG = P.ps("psG", [128, 1024])
    psGv = psG.ap().rearrange("p (a n) -> p a n", a=8)

    def c0(e, ds):
        e.dma_start(out=keysb.ap(),
                    in_=keys_in.rearrange("s p k -> p s k")).then_inc(ds, 16)
        e.dma_start(out=idf[:], in_=id_in).then_inc(ds, 16)
        e.dma_start(out=gr_loads[0], in_=gr_in).then_inc(ds, 16)
        e.dma_start(out=gr_loads[1], in_=br_in).then_inc(ds, 16)
    P.step("gpsimd", lambda e, ds: e.dma_start(
        out=idb[:], in_=id_in).then_inc(ds, 16), dma=1)
    P.step("gpsimd", lambda e: e.memset(onesf.ap(), 1.0))
    P.step("gpsimd", lambda e: e.memset(onesb.ap(), 1.0))
    P.step("gpsimd", lambda e, ds: e.dma_start(
        out=wqb.ap(), in_=wq_in.rearrange("(a p) n -> p a n", p=128))
        .then_inc(ds, 16), dma=1)

    for c in range(NCH_H):
        P.step("sync", lambda e, ds, c=c: e.dma_start(
            out=xt[:], in_=x1v[c]).then_inc(ds, 16), dma=1)
        _ln_steps(P, tv, xt, xn, sq16t, 1e-5, rms=False)
        P.step("vector", lambda e, c=c: e.tensor_copy(
            xn1tok[:, c, :], xn[:, :]))
        _tp8(P, xn1tok[:, c, :], psT, idb, lambda e, c=c: e.activation(
            xn1fm[:, :, 128 * c:128 * c + 128],
            psT.ap().rearrange("p (a n) -> p a n", a=8), AF.Copy))
        for g in range(4):
            def qmm(e, c=c, g=g):
                last = None
                for ks in range(8):
                    last = e.matmul(psS[:, :],
                                    xn1fm[:, ks, 128 * c:128 * c + 128],
                                    wqb[:, ks, 512 * g:512 * g + 512],
                                    start=(ks == 0), stop=(ks == 7))
                return last
            P.step("tensor", qmm)
            P.step("scalar", lambda e, c=c, g=g: e.activation(
                qq[:, c, 512 * g:512 * g + 512], psS[:, :], AF.Copy))
            P.step("tensor", lambda e, c=c, g=g: e.matmul(
                psU[:, :], onesb[:, :], qq[:, c, 512 * g:512 * g + 512],
                start=True, stop=True))
            if c == 0:
                P.step("vector", lambda e, g=g: e.tensor_copy(
                    ssum[:, 512 * g:512 * g + 512], psU[:, :]))
            else:
                P.step("vector", lambda e, g=g: e.tensor_tensor(
                    ssum[:, 512 * g:512 * g + 512],
                    ssum[:, 512 * g:512 * g + 512], psU[:, :], ALU.add))
            P.step("vector", lambda e, c=c, g=g: e.tensor_tensor(
                qsq[:, :], qq[:, c, 512 * g:512 * g + 512],
                qq[:, c, 512 * g:512 * g + 512], ALU.mult))
            P.step("tensor", lambda e: e.matmul(
                psV[:, :], onesf[:, :], qsq[:, :], start=True, stop=True))
            if c == 0:
                P.step("vector", lambda e, g=g: e.tensor_copy(
                    ssq[:, 512 * g:512 * g + 512], psV[:, :]))
            else:
                P.step("vector", lambda e, g=g: e.tensor_tensor(
                    ssq[:, 512 * g:512 * g + 512],
                    ssq[:, 512 * g:512 * g + 512], psV[:, :], ALU.add))
    s1s.close()

    s2s = ExitStack()
    grep = P.sb("grep", [128, 2048], scope=s2s)
    brep = P.sb("brep", [128, 2048], scope=s2s)
    arep = P.sb("arep", [128, 2048], scope=s2s)
    brp = P.sb("brp", [128, 2048], scope=s2s)
    qf = P.sb("qf", [128, 2048], scope=s2s)
    srA = P.sb("srA", [1, 2048], scope=s2s)
    srB = P.sb("srB", [1, 2048], scope=s2s)
    gr_loads = [grep.ap(), brep.ap()]
    P.step("sync", c0, dma=4)

    def st_out(e, ds):
        e.dma_start(out=cc_in[0:1, :], in_=ssum[0:1, :]).then_inc(ds, 16)
        e.dma_start(out=cc_in[1:2, :], in_=ssq[0:1, :]).then_inc(ds, 16)
    P.step("sync", st_out, dma=2)
    P.step("gpsimd", lambda e: e.collective_compute(
        "AllReduce", ALU.add,
        replica_groups=[[0, 1, 2, 3, 4, 5, 6, 7]],
        ins=[cc_in.tensor.ap().opt()],
        outs=[cc_out.tensor.ap().opt()]))
    def st_in(e, ds):
        e.dma_start(out=srA[:], in_=cc_out[0:1, :]).then_inc(ds, 16)
        e.dma_start(out=srB[:], in_=cc_out[1:2, :]).then_inc(ds, 16)
    P.step("sync", st_in, dma=2)
    def st_dbg(e, ds):
        e.dma_start(out=dst_out[0:1, :], in_=srA[:]).then_inc(ds, 16)
        e.dma_start(out=dst_out[1:2, :], in_=srB[:]).then_inc(ds, 16)
    P.step("sync", st_dbg, dma=2)
    for g in range(4):
        P.step("tensor", lambda e, g=g: e.matmul(
            psU[:, :], onesf[0:1, :], srA[0:1, 512 * g:512 * g + 512],
            start=True, stop=True))
        P.step("scalar", lambda e, g=g: e.activation(
            ssum[:, 512 * g:512 * g + 512], psU[:, :], AF.Copy))
        P.step("tensor", lambda e, g=g: e.matmul(
            psV[:, :], onesf[0:1, :], srB[0:1, 512 * g:512 * g + 512],
            start=True, stop=True))
        P.step("scalar", lambda e, g=g: e.activation(
            ssq[:, 512 * g:512 * g + 512], psV[:, :], AF.Copy))
    NT = 4 * TF
    P.step("vector", lambda e: e.tensor_scalar(
        ssum[:, :], ssum[:, :], 1.0 / NT, None, ALU.mult))
    P.step("vector", lambda e: e.tensor_scalar(
        ssq[:, :], ssq[:, :], 1.0 / NT, None, ALU.mult))
    P.step("vector", lambda e: e.tensor_tensor(
        arep[:, :], ssum[:, :], ssum[:, :], ALU.mult))
    P.step("vector", lambda e: e.tensor_tensor(
        ssq[:, :], ssq[:, :], arep[:, :], ALU.subtract))
    P.step("vector", lambda e: e.tensor_scalar(
        ssq[:, :], ssq[:, :], 1e-5, None, ALU.add))
    P.step("scalar", lambda e: e.activation(arep[:, :], ssq[:, :], AF.Ln))
    P.step("scalar", lambda e: e.activation(
        arep[:, :], arep[:, :], AF.Exp, scale=-0.5))
    P.step("vector", lambda e: e.tensor_tensor(
        arep[:, :], arep[:, :], grep[:, :], ALU.mult))
    P.step("vector", lambda e: e.tensor_tensor(
        brp[:, :], ssum[:, :], arep[:, :], ALU.mult))
    P.step("vector", lambda e: e.tensor_tensor(
        brp[:, :], brep[:, :], brp[:, :], ALU.subtract))
    for c in range(NCH_H):
        P.step("vector", lambda e, c=c: e.tensor_copy(qf[:, :], qq[:, c, :]))
        P.step("vector", lambda e: e.tensor_tensor(
            qf[:, :], qf[:, :], arep[:, :], ALU.mult))
        P.step("vector", lambda e: e.tensor_tensor(
            qf[:, :], qf[:, :], brp[:, :], ALU.add))
        P.step("vector", lambda e, c=c: e.tensor_copy(qq[:, c, :], qf[:, :]))
    s2s.close()

    sel64b = P.sb("sel64b", [128, 64, 128], BF16)
    iot = P.sb("iot", [128, 64])
    qfmc = P.sb("qfmc", [128, 16, 128])
    vals8 = P.sb("vals8", [128, 16, 8])
    idx8 = P.sb("idx8", [128, 16, 8], U16)
    idxf = P.sb("idxf", [128, 16, 8])
    t8 = P.sb("t8", [128, 8])
    t8b = P.sb("t8b", [128, 8])
    alls = P.sb("alls", [128, 64])
    alli = P.sb("alli", [128, 64])
    mk3 = P.sb("mk3", [128, 8, 64])
    scr3 = P.sb("scr3", [128, 8, 64])
    sca = P.sb("sca", [128, 64])
    smx = P.sb("smx", [128, 64])
    pkt = P.sb("pkt", [128, 8], U16)
    pkf = P.sb("pkf", [128, 8])
    v8t = P.sb("v8t", [128, 8])
    i8t = P.sb("i8t", [128, 8], U16)
    idxall = P.sb("idxall", [128, 64])
    idxpair = P.sb("idxpair", [128, 64], U32)
    hall = P.sb("hall", [128, 64])
    hg = P.sb("hg", [128, 64])
    coefp = P.sb("coefp", [128, 64])
    rhsall = P.sb("rhsall", [128, 128])
    wdr = P.sb("wdr", [128, 2, D])
    scrD = P.sb("scrD", [128, D])
    g16 = P.sb("g16", [128, D], BF16)
    g1t = P.sb("g1t", [128, D], BF16)
    s1t = P.sb("s1t", [128, D])

    def c2(e, ds):
        e.dma_start(out=sel64b.ap(), in_=sel_in).then_inc(ds, 16)
    P.step("gpsimd", c2, dma=1)
    P.step("sync", lambda e, ds: e.dma_start(
        out=iot[:], in_=iota_in).then_inc(ds, 16), dma=1)

    v1 = lambda ap: ap.rearrange("p (i u) -> p i u", u=1)
    vm = lambda ap, n: ap.rearrange("p (u i) -> p u i", u=1).broadcast_to(
        [128, n, ap.shape[-1]])

    if lim == "b":
        return P
    for c in range(NCH_H):
        if lim == "topk" and c > 0:
            break
        for hlf in range(2):
            _tp8(P, qq[:, c, 1024 * hlf:1024 * hlf + 1024], psT, idb,
                 lambda e, hlf=hlf: e.activation(
                     qfmc[:, 8 * hlf:8 * hlf + 8, :],
                     psT.ap().rearrange("p (a n) -> p a n", a=8), AF.Copy))
        for s in range(16):
            P.step("tensor", lambda e, s=s: e.matmul(
                psS[:, 0:256], qfmc[:, s, :], keysb[:, s, :],
                start=True, stop=True))
            P.step("vector", lambda e: e.max(v8t[:, :], psS[:, 0:256]))
            P.step("vector", lambda e: e.max_index(
                i8t[:, :], v8t[:, :], psS[:, 0:256]))
            P.step("vector", lambda e, s=s: e.tensor_copy(
                vals8[:, s, :], v8t[:, :]))
            P.step("vector", lambda e, s=s: e.tensor_copy(
                idx8[:, s, :], i8t[:, :]))
        P.step("vector", lambda e: e.tensor_copy(idxf.ap(), idx8.ap()))
        for h in range(8):
            sx = v1(vals8[:, h, :]).broadcast_to([128, 8, 8])
            sy = vm(vals8[:, 8 + h, :], 8)
            iy = vm(idxf[:, 8 + h, :], 8)
            av3 = alls.ap().rearrange("p (i j) -> p i j", j=8)
            ai3 = alli.ap().rearrange("p (i j) -> p i j", j=8)
            P.step("vector", lambda e, sx=sx, sy=sy, av3=av3: e.tensor_tensor(
                av3, sx, sy, ALU.add))
            P.step("vector", lambda e, h=h: e.tensor_scalar(
                t8[:, :], idxf[:, h, :], 256.0, None, ALU.mult))
            tx = v1(t8.ap()).broadcast_to([128, 8, 8])
            P.step("vector", lambda e, tx=tx, iy=iy, ai3=ai3: e.tensor_tensor(
                ai3, tx, iy, ALU.add))
            P.step("vector", lambda e: e.max(v8t[:, :], alls[:, :]))
            P.step("vector", lambda e: e.max_index(
                pkt[:, :], v8t[:, :], alls[:, :]))
            P.step("vector", lambda e, h=h: e.tensor_copy(
                sca[:, 8 * h:8 * h + 8], v8t[:, :]))
            P.step("vector", lambda e: e.tensor_copy(pkf.ap(), pkt.ap()))
            iob = vm(iot.ap(), 8)
            pkb = v1(pkf.ap()).broadcast_to([128, 8, 64])
            alb = vm(alli.ap(), 8)
            P.step("vector", lambda e, iob=iob, pkb=pkb: e.tensor_tensor(
                mk3.ap(), iob, pkb, ALU.is_equal))
            P.step("vector", lambda e, alb=alb: e.tensor_tensor(
                scr3.ap(), mk3.ap(), alb, ALU.mult))
            P.step("vector", lambda e, h=h: e.tensor_reduce(
                idxall[:, 8 * h:8 * h + 8], scr3.ap(), axis=AXX, op=ALU.add))
        P.step("sync", lambda e, ds, c=c: e.dma_start(
            out=didx_out[:, c, :], in_=idxall[:]).then_inc(ds, 16), dma=1)
        sc3 = sca.ap().rearrange("p (i j) -> p i j", j=8)
        P.step("vector", lambda e, sc3=sc3: e.tensor_reduce(
            t8[:, :], sc3, axis=AXX, op=ALU.max))
        P.step("vector", lambda e, sc3=sc3: e.tensor_tensor(
            sc3, sc3, v1(t8.ap()).broadcast_to([128, 8, 8]), ALU.subtract))
        P.step("scalar", lambda e: e.activation(sca[:, :], sca[:, :], AF.Exp))
        P.step("vector", lambda e, sc3=sc3: e.tensor_reduce(
            t8[:, :], sc3, axis=AXX, op=ALU.add))
        P.step("vector", lambda e: e.reciprocal(t8b[:, :], t8[:, :]))
        P.step("vector", lambda e, sc3=sc3: e.tensor_tensor(
            smx.ap().rearrange("p (i j) -> p i j", j=8), sc3,
            v1(t8b.ap()).broadcast_to([128, 8, 8]), ALU.mult))
        P.step("tensor", lambda e: e.transpose(
            psS[0:64, 0:128], idxall[:, :], idf[:, :]))
        P.step("tensor", lambda e: e.transpose(
            psS[0:64, 128:256], smx[:, :], idf[:, :]))
        ev = lambda base, k: psS.ap()[0:64, base:base + 128].rearrange(
            "p (i two) -> p i two", two=2)[:, :, k:k + 1]
        P.step("vector", lambda e: e.tensor_copy(
            v1(idxpair.ap()[0:64, :]), ev(0, 0)))
        P.step("vector", lambda e: e.tensor_copy(
            v1(idxpair.ap()[64:128, :]), ev(0, 1)))
        P.step("vector", lambda e: e.tensor_copy(
            v1(coefp.ap()[0:64, :]), ev(128, 0)))
        P.step("vector", lambda e: e.tensor_copy(
            v1(coefp.ap()[64:128, :]), ev(128, 1)))
        k_idx = P.main.k
        if lim == "topk":
            dv = P.dout("d_vals", [128, 16, 8])
            di8 = P.dout("d_idx8", [128, 16, 8], U16)
            dal = P.dout("d_alli", [128, 64])
            dals = P.dout("d_alls", [128, 64])
            dsc = P.dout("d_sca", [128, 64])
            dpk = P.dout("d_pkf", [128, 8])
            dsm = P.dout("d_smx", [128, 64])
            def dmp(e, ds):
                e.dma_start(out=dv, in_=vals8.ap()).then_inc(ds, 16)
                e.dma_start(out=di8, in_=idx8.ap()).then_inc(ds, 16)
                e.dma_start(out=dal, in_=alli[:]).then_inc(ds, 16)
                e.dma_start(out=dals, in_=alls[:]).then_inc(ds, 16)
                e.dma_start(out=dsc, in_=sca[:]).then_inc(ds, 16)
                e.dma_start(out=dpk, in_=pkf[:]).then_inc(ds, 16)
                e.dma_start(out=dsm, in_=smx[:]).then_inc(ds, 16)
            P.step("sync", dmp, dma=7)
            break
        dot_k = {}
        for i in range(64):
            kfree = dot_k.get(i - 2, k_idx)
            def gth(e, ds, c=c, i=i):
                e.indirect_dma_start(
                    out=wdr[:, i % 2, :], out_offset=None,
                    in_=wd_in, in_offset=bass.IndirectOffsetOnAxis(
                        ap=idxpair[:, i:i + 1], axis=0)).then_inc(ds, 16)
            P.step("gpsimd", gth, dma=1, chain=P.aux,
                   also=[(P.main, "k", max(kfree, k_idx))])
            gd = P.aux.d
            def xpm(e, c=c, i=i):
                e.matmul(psQd[:, 0:512], sel64b[:, i, :],
                         xn1tok[:, c, 0:512], start=True, stop=True)
                return e.matmul(psQd[:, 512:1024], sel64b[:, i, :],
                                xn1tok[:, c, 512:1024], start=True, stop=True)
            P.step("tensor", xpm)
            P.step("vector", lambda e, i=i: e.tensor_tensor(
                scrD.ap(), wdr[:, i % 2, :], psQd.ap(), ALU.mult),
                also=[(P.aux, "d", gd)])
            P.step("vector", lambda e, i=i: e.tensor_reduce(
                hall[:, i:i + 1], scrD.ap(), axis=AXX, op=ALU.add))
            dot_k[i] = P.main.k
        P.step("scalar", lambda e: e.activation(
            hg[:, :], hall[:, :], AF.Gelu_apprx_tanh))
        P.step("vector", lambda e: e.tensor_tensor(
            coefp[:, :], coefp[:, :], hg[:, :], ALU.mult))
        P.step("vector", lambda e: e.memset(rhsall.ap(), 0.0))
        rv = lambda p0, k: rhsall.ap()[p0:p0 + 64, :].rearrange(
            "p (i two) -> p i two", two=2)[:, :, k:k + 1]
        P.step("vector", lambda e: e.tensor_copy(
            rv(0, 0), v1(coefp.ap()[0:64, :])))
        P.step("vector", lambda e: e.tensor_copy(
            rv(64, 1), v1(coefp.ap()[64:128, :])))
        k_rhs = P.main.k
        if lim == "loop1":
            break
        up_k = {}
        for i in range(64):
            kfree = up_k.get(i - 2, k_rhs)
            def gthu(e, ds, c=c, i=i):
                e.indirect_dma_start(
                    out=wdr[:, i % 2, :], out_offset=None,
                    in_=wu_in, in_offset=bass.IndirectOffsetOnAxis(
                        ap=idxpair[:, i:i + 1], axis=0)).then_inc(ds, 16)
            P.step("gpsimd", gthu, dma=1, chain=P.aux,
                   also=[(P.main, "k", max(kfree, k_rhs))])
            gd = P.aux.d
            def upm(e, i=i):
                last = None
                for dsl in range(8):
                    last = e.matmul(psGv[:, dsl, 2 * i:2 * i + 2],
                                    wdr[:, i % 2, 128 * dsl:128 * dsl + 128],
                                    rhsall[:, 2 * i:2 * i + 2],
                                    start=True, stop=True)
                return last
            P.step("tensor", upm, also=[(P.aux, "d", gd)])
            up_k[i] = P.main.k
        P.step("scalar", lambda e: e.activation(g16.ap(), psG.ap(), AF.Copy))
        _tp8(P, g16, psT, idb, lambda e: e.activation(
            g1t.ap().rearrange("p (a n) -> p a n", a=8),
            psT.ap().rearrange("p (a n) -> p a n", a=8), AF.Copy))
        P.step("sync", lambda e, ds, c=c: e.dma_start(
            out=dg1v[c], in_=g1t[:]).then_inc(ds, 16), dma=1)
        def cld(e, ds, c=c):
            e.dma_start(out=s1t[:], in_=s1v[c]).then_inc(ds, 16)
            e.dma_start(out=xt[:], in_=x1v[c]).then_inc(ds, 16)
        P.step("sync", cld, dma=2)
        P.step("scalar", lambda e: e.activation(scrD.ap(), g1t.ap(), AF.Silu))
        P.step("vector", lambda e: e.tensor_tensor(
            s1t[:, :], s1t[:, :], scrD[:, :], ALU.mult))
        P.step("vector", lambda e: e.tensor_tensor(
            s1t[:, :], s1t[:, :], xt[:, :], ALU.add))
        P.step("sync", lambda e, ds, c=c: e.dma_start(
            out=outv[c], in_=s1t[:]).then_inc(ds, 16), dma=1)
        if lim and lim.startswith("chunk") and c == int(lim[5:]) - 1:
            break

    return P


def _prep_d(I, x1_my_list, s1_my_list):
    keys_r = np.ascontiguousarray(
        I["L1_ffn_keys"].transpose(2, 0, 3, 1).reshape(16, 128, 256))
    sel64 = np.zeros((128, 64, 128), np.float32)
    for i in range(64):
        sel64[2 * i, i, 0:64] = 1.0
        sel64[2 * i + 1, i, 64:128] = 1.0
    common = dict(
        w_q=I["L1_ffn_w_q"],
        bn_gamma_rep=np.ascontiguousarray(
            np.broadcast_to(I["L1_ffn_bn_gamma"], (128, 2048))),
        bn_beta_rep=np.ascontiguousarray(
            np.broadcast_to(I["L1_ffn_bn_beta"], (128, 2048))),
        keys_r=keys_r,
        down_embed=I["L1_ffn_down_embed"], up_embed=I["L1_ffn_up_embed"],
        sel64=sel64,
        iota64=np.ascontiguousarray(
            np.broadcast_to(np.arange(64, dtype=np.float32), (128, 64))),
        ident=np.eye(128, dtype=np.float32),
    )
    maps = []
    for c in range(8):
        m = dict(common)
        m["x1_my"] = np.ascontiguousarray(x1_my_list[c], dtype=np.float32)
        m["s1_my"] = np.ascontiguousarray(s1_my_list[c], dtype=np.float32)
        maps.append(m)
    return maps


def _run(P, maps):
    nc = P.build()
    return run_bass_kernel_spmd(nc, maps, core_ids=list(range(8)))


def kernel(**inputs):
    I = {k: np.ascontiguousarray(np.asarray(v, dtype=np.float32))
         for k, v in inputs.items()}

    mapsA = _prep_host(inputs)
    ncA, _ = build_program(DBG, "mixer")
    resA = run_bass_kernel_spmd(ncA, mapsA, core_ids=list(range(8)))
    og_full = []
    for b in range(4):
        og = np.asarray(resA.results[2 * b]["og_out"])
        og_full.append(np.concatenate([og[0:TF, :], og[TF:2 * TF, :]], axis=1))

    mapsB = _prep_b(I)
    for c in range(8):
        b, th = c // 2, c % 2
        mapsB[c]["og_my"] = np.ascontiguousarray(
            og_full[b][th * TH:(th + 1) * TH])
    resB = _run(build_b(), mapsB)
    x1_list = [np.asarray(resB.results[c]["x1"]) for c in range(8)]
    x1_full = np.stack([np.concatenate([x1_list[2 * b], x1_list[2 * b + 1]], 0)
                        for b in range(4)])

    mapsC = _prep_c(I, x1_full)
    resC = _run(build_c(), mapsC)
    s1_list = [np.asarray(resC.results[c]["s1"]) for c in range(8)]

    mapsD = _prep_d(I, x1_list, s1_list)
    resD = _run(build_d(), mapsD)
    full = np.zeros((4, TF, D), np.float32)
    for c in range(8):
        b, th = c // 2, c % 2
        full[b, th * TH:(th + 1) * TH] = np.asarray(resD.results[c]["xout"])
    return full
